# version 09
# baseline (speedup 1.0000x reference)
import math
import numpy as np
import ml_dtypes
import concourse.bass as bass
import concourse.mybir as mybir
from concourse.bass_utils import run_bass_kernel_spmd

F32 = mybir.dt.float32
BF16 = mybir.dt.bfloat16
I32 = mybir.dt.int32
AF = mybir.ActivationFunctionType
OP = mybir.AluOpType

H = 1024
L = 16384
NCORE = 8
LC = L // NCORE
T = 128
NOWN = LC // T
NPRE = (L - LC) // T
NSUP = NPRE // 8
NST = 64
E = 8
F = 4096
Q2 = 32
EPS = 1e-5
JIT2 = 0.02
TWO_PI = float(np.float32(2.0 * math.pi))
TWO_PI_ERR = float(np.float64(TWO_PI) - 2.0 * math.pi)
HALF_PI = 0.5 * math.pi
PI_C = float(np.float32(math.pi))

ENGS = ("sync", "gpsimd", "scalar", "vector", "tensor")
DEBUG_TAPS = False


class Prog:

    def __init__(self, semnames):
        self.q = {e: [] for e in ENGS}
        self.cnt = {s: 0 for s in semnames}
        self.lastwait = {e: {s: -1 for s in semnames} for e in ENGS}

    def emit(self, eng, fn, waits=(), inc=None):
        w = []
        for s, v in waits:
            assert v <= self.cnt[s], f"wait on future: {s} {v} > {self.cnt[s]}"
            if v > self.lastwait[eng][s] or eng == "gpsimd":
                w.append((s, v))
                self.lastwait[eng][s] = max(v, self.lastwait[eng][s])
        if inc is not None:
            self.cnt[inc[0]] += inc[1]
        self.q[eng].append((w, fn, inc))

    def c(self, sem):
        return self.cnt[sem]


def build_program(limit=99, sub=99, dlim=E):
    nc = bass.Bass("TRN2", target_bir_lowering=False)

    d_x = nc.dram_tensor("x_shift", [L, H], F32, kind="ExternalInput").ap()
    d_mask = nc.dram_tensor("mask_pre", [128, NPRE], F32, kind="ExternalInput").ap()
    d_gw = nc.dram_tensor("gate_wT", [H, E], F32, kind="ExternalInput").ap()
    d_w1 = nc.dram_tensor("w1T", [E, H, F], F32, kind="ExternalInput").ap()
    d_w2 = nc.dram_tensor("w2T", [E, F, H], F32, kind="ExternalInput").ap()
    d_w3 = nc.dram_tensor("w3T", [E, H, F], F32, kind="ExternalInput").ap()
    d_cr = nc.dram_tensor("CR", [H, NST], F32, kind="ExternalInput").ap()
    d_ci = nc.dram_tensor("CI", [H, NST], F32, kind="ExternalInput").ap()
    d_gamma = nc.dram_tensor("gamma", [H], F32, kind="ExternalInput").ap()
    d_beta = nc.dram_tensor("beta", [H], F32, kind="ExternalInput").ap()
    d_D = nc.dram_tensor("Dvec", [H], F32, kind="ExternalInput").ap()
    d_lr = nc.dram_tensor("LR", [NST], F32, kind="ExternalInput").ap()
    d_li = nc.dram_tensor("LI", [NST], F32, kind="ExternalInput").ap()
    d_out = nc.dram_tensor("out", [LC, H], F32, kind="ExternalOutput").ap()
    if DEBUG_TAPS:
        d_tap1 = nc.dram_tensor("tap1", [128, 16], F32, kind="ExternalOutput").ap()
        d_tap2 = nc.dram_tensor("tap2", [128, 1024], F32, kind="ExternalOutput").ap()
        d_tap3 = nc.dram_tensor("tap3", [128, 1024], F32, kind="ExternalOutput").ap()
        d_tap4 = nc.dram_tensor("tap4", [128, 128], F32, kind="ExternalOutput").ap()
        d_tap5 = nc.dram_tensor("tap5", [128, 128], F32, kind="ExternalOutput").ap()

    A = nc.alloc_sbuf_tensor
    ACC = A("ACC", [128, NOWN * H], F32)
    XNFM = A("XNFM", [128, 8 * LC], BF16)
    POOLB = A("POOLB", [128, 32768], BF16)
    POOLF = A("POOLF", [128, 4224], F32)
    GSB = A("GSB", [128, 2 * 4096], BF16)
    YD32 = A("YD32", [128, 2176], F32)
    SILT = A("SILT", [128, 2 * 512], BF16)
    K32 = A("K32", [128, 8 * Q2 + NOWN * E], F32)
    KI32 = A("KI32", [128, 1], I32)

    XN2TM = POOLB[:, 0:16384]
    W1B = POOLB[:, 0:12288]
    W3B = POOLB[:, 12288:24576]
    W2B = POOLB[:, 24576:32768]
    PRECHB = XN2TM[:, 0:1024]
    CHB = POOLB[:, 16384:17408]
    XNLOF = POOLB[:, 17408:18432]
    OWNB = POOLB[:, 18432:20480]
    A1EQ = OWNB[:, 0:1024]
    CNR = POOLB[0:NST, 20480:21504]
    CNI = POOLB[0:NST, 21504:22528]
    G1S = POOLB[0:NST, 22528:23296]
    A1PK = POOLB[:, 23296:23424]
    A2L = POOLB[:, 23424:23808]
    GATEH = POOLB[:, 23808:23872]
    GATEL = POOLB[:, 23872:23936]
    MASKB = POOLB[:, 23936:24048]
    IDB = POOLB[:, 24048:24176]
    P1RB = POOLB[0:NST, 24176:24208]
    P1IB = POOLB[0:NST, 24208:24240]
    G1RB = POOLB[0:NST, 24240:24368]
    G1INB = POOLB[0:NST, 24368:24496]
    XF2 = POOLB[:, 24496:25520]
    DXF2 = POOLB[:, 25520:26544]
    OB2 = POOLB[:, 26544:27568]
    STG1 = CNR[:, :]
    STG2 = CNI[:, :]
    CSR = XNLOF[0:NST, :]
    CSI = CHB[0:NST, :]
    CNINB = CSI[:, :]

    VTMP = POOLF[:, 0:1024]
    XSTG = POOLF[:, 1024:3072]
    SQT = POOLF[:, 3072:4096]
    EQC = POOLF[0:NST, 4096:4100]
    GCOL = POOLF[:, 4100:4108]
    BCOL = POOLF[:, 4108:4116]
    LAM = POOLF[0:NST, 4116:4132]
    PTAB = XSTG[:, 0:512]
    P1R = PTAB[0:NST, 0:128]
    P1I = PTAB[0:NST, 128:256]
    A1R = PTAB[0:NST, 256:384]
    A1I = PTAB[0:NST, 384:512]
    IDF = YD32[:, 2048:2176]
    STAT = POOLF[:, 4132:4148]
    DROW = POOLF[:, 4148:4156]
    SRS = POOLF[:, 4156:4172]
    EQ8 = POOLF[:, 4172:4214]
    GATEF = VTMP[:, 0:64]
    GBt = XSTG[:, 0:1024]
    BBt = XSTG[:, 1024:2048]
    RSC = XSTG[:, 0:1152]
    RSM = SQT[:, 128:256]
    WTOKSB = SQT[:, 256:384]
    SCORES = SQT[:, 0:128]
    SR_ = XSTG[0:NST, 0:1024]
    SI_ = XSTG[0:NST, 1024:2048]
    WTOKD = K32[:, 8 * Q2:8 * Q2 + NOWN * E]
    CR_STG = YD32[:, 0:512]
    CI_STG = YD32[:, 512:1024]
    SINR = YD32[:, 0:1024]
    CX2R = GSB[0:NST, 0:3072]
    CX2I = GSB[0:NST, 3072:6144]
    X2FM = SILT[:, :]

    PS = [nc.alloc_psum_tensor(f"PSF{i}", [128, 512], F32) for i in range(7)]
    PBT = nc.alloc_psum_tensor("PBT", [128, 1024], BF16)

    semnames = ["sX", "sWD", "sSC", "sVE", "sTE", "sST", "sGP",
                "sXa", "sXb", "sW0", "sW1", "sW2", "sWM", "sV2a", "sV2b"]
    sems = {n: nc.alloc_semaphore(n) for n in semnames}
    P = Prog(semnames)
    em = P.emit
    V, SC, TE, SY, GP = "vector", "scalar", "tensor", "sync", "gpsimd"

    def dma(eng, out_ap, in_ap, waits=(), sem="sX", slow=False):
        em(eng, lambda e, o=out_ap, i=in_ap, s=slow: e.dma_start(
            out=o, in_=i, allow_slow_non_contiguous=s),
           waits=waits, inc=(sem, 16))

    def finish():
        em(SY, lambda s: s.dma_start(out=d_out.rearrange("(j p) h -> p j h", p=128),
                                     in_=ACC[:, :].rearrange("p (j h) -> p j h", h=H)),
           waits=[("sVE", P.c("sVE")), ("sTE", P.c("sTE")), ("sSC", P.c("sSC")),
                  ("sWD", P.c("sWD")), ("sX", P.c("sX")), ("sGP", P.c("sGP")),
                  ("sXa", P.c("sXa")), ("sXb", P.c("sXb")), ("sW0", P.c("sW0")),
                  ("sW1", P.c("sW1")), ("sW2", P.c("sW2")), ("sWM", P.c("sWM")),
                  ("sV2a", P.c("sV2a")), ("sV2b", P.c("sV2b"))],
           inc=("sST", 16))
        em(SY, lambda s: s.wait_ge(sems["sST"], P.c("sST")))
        with nc.Block() as block:
            def mk(engname):
                def run(eng):
                    for w, fn, inc in P.q[engname]:
                        for s, v in w:
                            eng.wait_ge(sems[s], v)
                        ins = fn(eng)
                        if inc is not None:
                            assert ins is not None, f"{engname} op needs inst for inc"
                            ins.then_inc(sems[inc[0]], inc[1])
                return run
            block.sync(mk("sync"))
            block.gpsimd(mk("gpsimd"))
            block.scalar(mk("scalar"))
            block.vector(mk("vector"))
            block.tensor(mk("tensor"))
        return nc
    finish_hook = [finish]
    dma(SY, GCOL[:], d_gamma.rearrange("(hc p) -> p hc", p=128), slow=True)
    dma(SY, BCOL[:], d_beta.rearrange("(hc p) -> p hc", p=128), slow=True)
    dma(SY, DROW[:], d_D.rearrange("(hc p) -> p hc", p=128), slow=True)
    dma(SY, LAM[:, 0:1], d_lr.rearrange("(p one) -> p one", one=1), slow=True)
    dma(SY, LAM[:, 1:2], d_li.rearrange("(p one) -> p one", one=1), slow=True)
    dma(SY, GATEF[:].rearrange("p (hc e) -> p hc e", e=E),
        d_gw.rearrange("(hc p) e -> p hc e", p=128), slow=True)
    dma(SY, CR_STG.rearrange("p (hc n) -> p hc n", n=NST),
        d_cr.rearrange("(hc p) n -> p hc n", p=128))
    dma(SY, CI_STG.rearrange("p (hc n) -> p hc n", n=NST),
        d_ci.rearrange("(hc p) n -> p hc n", p=128))
    dma(SY, GBt, d_gamma.unsqueeze(0).partition_broadcast(128))
    dma(SY, BBt, d_beta.unsqueeze(0).partition_broadcast(128))
    ld_small = P.c("sX")
    dma(GP, MASKB[:], d_mask, sem="sWM")
    ld_maskb = P.c("sWM")

    em(V, lambda v: (v.memset(IDB[:], 1.0), v.memset(IDF[:], 1.0),
                     v.drain())[-1], inc=("sVE", 1))
    ve_idm = P.c("sVE")
    em(GP, lambda g: g.affine_select(IDB[:], IDB[:], [[-1, 128]], OP.is_equal,
                                     0.0, base=0, channel_multiplier=1),
       waits=[("sVE", ve_idm)], inc=("sGP", 1))
    em(GP, lambda g: g.affine_select(IDF[:], IDF[:], [[-1, 128]], OP.is_equal,
                                     0.0, base=0, channel_multiplier=1),
       waits=[("sVE", ve_idm)], inc=("sGP", 1))
    gp_ident_gp = P.c("sGP")

    if sub <= 1:
        return finish()
    wsm = [("sX", ld_small)]
    em(V, lambda v: (v.tensor_copy(GATEH[:], GATEF[:]), v.drain())[0], waits=wsm)
    em(V, lambda v: (v.tensor_sub(GATEL[:], GATEF[:], GATEH[:]), v.drain())[0])
    ld_dgb = P.c("sX")

    em(SC, lambda s: s.activation(LAM[:, 2:3], LAM[:, 0:1], AF.Exp), waits=wsm)
    em(SC, lambda s: s.activation(LAM[:, 3:4], LAM[:, 1:2], AF.Exp), inc=("sSC", 1))
    em(V, lambda v: (v.tensor_scalar(LAM[:, 2:3], LAM[:, 2:3], -1.0, None, OP.mult), v.drain())[0],
       waits=[("sSC", P.c("sSC"))])
    em(V, lambda v: (v.tensor_copy(LAM[:, 8:9], LAM[:, 2:3]), v.drain())[0], inc=("sVE", 1))
    em(SC, lambda s: s.activation(LAM[:, 9:10], LAM[:, 8:9], AF.Exp),
       waits=[("sVE", P.c("sVE"))], inc=("sSC", 1))
    em(V, lambda v: (v.tensor_scalar(LAM[:, 10:11], LAM[:, 3:4], 1.0 / TWO_PI, 0.5,
                                     OP.mult, OP.add), v.drain())[0],
       waits=[("sSC", P.c("sSC"))])
    em(V, lambda v: (v.tensor_copy(KI32[0:NST, 0:1], LAM[:, 10:11]), v.drain())[0])
    em(V, lambda v: (v.tensor_copy(LAM[:, 10:11], KI32[0:NST, 0:1]), v.drain())[0])
    em(V, lambda v: (v.scalar_tensor_tensor(LAM[:, 12:13], LAM[:, 10:11], -TWO_PI,
                                           LAM[:, 3:4], OP.mult, OP.add), v.drain())[0])
    em(V, lambda v: (v.scalar_tensor_tensor(LAM[:, 12:13], LAM[:, 10:11], TWO_PI_ERR,
                                           LAM[:, 12:13], OP.mult, OP.add), v.drain())[0])
    em(V, lambda v: (v.tensor_scalar(LAM[:, 11:12], LAM[:, 12:13], PI_C, None, OP.is_gt), v.drain())[0])
    em(V, lambda v: (v.scalar_tensor_tensor(LAM[:, 12:13], LAM[:, 11:12], -TWO_PI,
                                           LAM[:, 12:13], OP.mult, OP.add), v.drain())[0])
    em(V, lambda v: (v.scalar_tensor_tensor(LAM[:, 12:13], LAM[:, 11:12], TWO_PI_ERR,
                                           LAM[:, 12:13], OP.mult, OP.add), v.drain())[0])
    em(V, lambda v: (v.tensor_scalar(LAM[:, 11:12], LAM[:, 12:13], -1.0, PI_C,
                                     OP.mult, OP.is_gt), v.drain())[0])
    em(V, lambda v: (v.scalar_tensor_tensor(LAM[:, 12:13], LAM[:, 11:12], TWO_PI,
                                           LAM[:, 12:13], OP.mult, OP.add), v.drain())[0])
    em(V, lambda v: (v.scalar_tensor_tensor(LAM[:, 12:13], LAM[:, 11:12], -TWO_PI_ERR,
                                           LAM[:, 12:13], OP.mult, OP.add), v.drain())[0])
    em(V, lambda v: (v.tensor_scalar(LAM[:, 13:14], LAM[:, 12:13], HALF_PI, None, OP.add), v.drain())[0])
    em(V, lambda v: (v.tensor_scalar(LAM[:, 11:12], LAM[:, 13:14], PI_C, None, OP.is_gt), v.drain())[0])
    em(V, lambda v: (v.scalar_tensor_tensor(LAM[:, 13:14], LAM[:, 11:12], -TWO_PI,
                                           LAM[:, 13:14], OP.mult, OP.add), v.drain())[0])
    em(V, lambda v: (v.scalar_tensor_tensor(LAM[:, 13:14], LAM[:, 11:12], TWO_PI_ERR,
                                           LAM[:, 13:14], OP.mult, OP.add), v.drain())[0], inc=("sVE", 1))
    em(SC, lambda s: s.activation(LAM[:, 10:11], LAM[:, 12:13], AF.Sin),
       waits=[("sVE", P.c("sVE"))])
    em(SC, lambda s: s.activation(LAM[:, 11:12], LAM[:, 13:14], AF.Sin), inc=("sSC", 1))
    em(V, lambda v: (v.tensor_mul(LAM[:, 4:5], LAM[:, 9:10], LAM[:, 11:12]), v.drain())[0],
       waits=[("sSC", P.c("sSC"))])
    em(V, lambda v: (v.tensor_mul(LAM[:, 5:6], LAM[:, 9:10], LAM[:, 10:11]), v.drain())[0])

    def q_ops(v):
        v.tensor_scalar(LAM[:, 8:9], LAM[:, 4:5], -1.0, None, OP.add)
        v.drain()
        v.tensor_mul(LAM[:, 9:10], LAM[:, 2:3], LAM[:, 2:3])
        v.drain()
        v.tensor_mul(LAM[:, 10:11], LAM[:, 3:4], LAM[:, 3:4])
        v.drain()
        v.tensor_add(LAM[:, 9:10], LAM[:, 9:10], LAM[:, 10:11])
        v.drain()
        v.reciprocal(LAM[:, 9:10], LAM[:, 9:10])
        v.drain()
        v.tensor_mul(LAM[:, 10:11], LAM[:, 8:9], LAM[:, 2:3])
        v.drain()
        v.tensor_mul(LAM[:, 11:12], LAM[:, 5:6], LAM[:, 3:4])
        v.drain()
        v.tensor_add(LAM[:, 10:11], LAM[:, 10:11], LAM[:, 11:12])
        v.drain()
        v.tensor_mul(LAM[:, 6:7], LAM[:, 10:11], LAM[:, 9:10])
        v.drain()
        v.tensor_mul(LAM[:, 10:11], LAM[:, 5:6], LAM[:, 2:3])
        v.drain()
        v.tensor_mul(LAM[:, 11:12], LAM[:, 8:9], LAM[:, 3:4])
        v.drain()
        v.tensor_sub(LAM[:, 10:11], LAM[:, 10:11], LAM[:, 11:12])
        v.drain()
        v.tensor_mul(LAM[:, 7:8], LAM[:, 10:11], LAM[:, 9:10])
        return v.drain()
    em(V, q_ops)

    def cxsq_running(v):
        v.tensor_mul(SQT[0:NST, 0:1], LAM[:, 14:15], LAM[:, 14:15])
        v.tensor_mul(SQT[0:NST, 1:2], LAM[:, 15:16], LAM[:, 15:16])
        v.tensor_mul(SQT[0:NST, 2:3], LAM[:, 14:15], LAM[:, 15:16])
        v.drain()
        v.tensor_sub(LAM[:, 14:15], SQT[0:NST, 0:1], SQT[0:NST, 1:2])
        v.tensor_scalar(LAM[:, 15:16], SQT[0:NST, 2:3], 2.0, None, OP.mult)
        return v.drain()

    def p1_build(v):
        v.memset(P1R[:, 0:1], 1.0)
        v.drain()
        v.memset(P1I[:, 0:1], 0.0)
        v.drain()
        v.tensor_copy(LAM[:, 14:15], LAM[:, 4:5])
        v.drain()
        v.tensor_copy(LAM[:, 15:16], LAM[:, 5:6])
        v.drain()
        blk = 1
        while blk < T:
            m = min(blk, T - blk)
            v.tensor_scalar(SQT[0:NST, 0:m], P1R[:, 0:m], LAM[:, 14:15], None, OP.mult)
            v.drain()
            v.tensor_scalar(SQT[0:NST, 256:256 + m], P1I[:, 0:m], LAM[:, 15:16], None, OP.mult)
            v.drain()
            v.tensor_sub(P1R[:, blk:blk + m], SQT[0:NST, 0:m], SQT[0:NST, 256:256 + m])
            v.drain()
            v.tensor_scalar(SQT[0:NST, 0:m], P1R[:, 0:m], LAM[:, 15:16], None, OP.mult)
            v.drain()
            v.tensor_scalar(SQT[0:NST, 256:256 + m], P1I[:, 0:m], LAM[:, 14:15], None, OP.mult)
            v.drain()
            v.tensor_add(P1I[:, blk:blk + m], SQT[0:NST, 0:m], SQT[0:NST, 256:256 + m])
            v.drain()
            cxsq_running(v)
            blk *= 2
        v.tensor_copy(EQC[:, 0:1], LAM[:, 14:15])
        v.drain()
        v.tensor_copy(EQC[:, 1:2], LAM[:, 15:16])
        v.drain()
        v.tensor_scalar(EQC[:, 2:3], LAM[:, 15:16], -1.0, None, OP.mult)
        return v.drain()
    em(V, p1_build)

    def a1_build(v):
        for c in range(T):
            v.tensor_copy(A1R[:, c:c + 1], P1R[:, T - 1 - c:T - c])
            v.tensor_copy(A1I[:, c:c + 1], P1I[:, T - 1 - c:T - c])
        return v.drain()
    em(V, a1_build)

    def g1_build(v):
        v.tensor_scalar(SQT[0:NST, 0:T], P1R[:, :], LAM[:, 4:5], None, OP.mult)
        v.tensor_scalar(SQT[0:NST, 256:256 + T], P1I[:, :], LAM[:, 5:6], None, OP.mult)
        v.drain()
        v.tensor_sub(SQT[0:NST, 512:512 + T], SQT[0:NST, 0:T], SQT[0:NST, 256:256 + T])
        v.drain()
        v.tensor_copy(G1RB[:, :], SQT[0:NST, 512:512 + T])
        v.drain()
        v.tensor_scalar(SQT[0:NST, 0:T], P1R[:, :], LAM[:, 5:6], None, OP.mult)
        v.tensor_scalar(SQT[0:NST, 256:256 + T], P1I[:, :], LAM[:, 4:5], None, OP.mult)
        v.drain()
        v.tensor_add(SQT[0:NST, 512:512 + T], SQT[0:NST, 0:T], SQT[0:NST, 256:256 + T])
        v.drain()
        v.tensor_scalar(SQT[0:NST, 512:512 + T], SQT[0:NST, 512:512 + T], -1.0, None, OP.mult)
        v.drain()
        v.tensor_copy(G1INB[:, :], SQT[0:NST, 512:512 + T])
        return v.drain()
    em(V, g1_build)

    def g1s_build(v):
        v.memset(G1S[:, :], 0.0)
        ins = None
        v.drain()
        for m in (1, 2, 3):
            sh = 32 * m
            v.tensor_copy(G1S[:, (m - 1) * T + sh:m * T], G1RB[:, 0:T - sh])
            v.tensor_copy(G1S[:, (2 + m) * T + sh:(3 + m) * T], G1INB[:, 0:T - sh])
        return v.drain()
    em(V, g1s_build)

    def eq8_build(v):
        v.tensor_copy(LAM[:, 14:15], EQC[:, 0:1])
        v.drain()
        v.tensor_copy(LAM[:, 15:16], EQC[:, 1:2])
        v.drain()
        for _ in range(3):
            cxsq_running(v)
        v.memset(EQ8[0:NST, 13:14], 1.0)
        v.drain()
        v.memset(EQ8[0:NST, NSUP + 13:NSUP + 14], 0.0)
        v.drain()
        for p in range(1, NSUP):
            cdst, csrc = 13 - p, 14 - p
            v.tensor_scalar(SQT[0:NST, 0:1], EQ8[0:NST, csrc:csrc + 1], LAM[:, 14:15], None, OP.mult)
            v.drain()
            v.tensor_scalar(SQT[0:NST, 1:2], EQ8[0:NST, NSUP + csrc:NSUP + csrc + 1], LAM[:, 15:16], None, OP.mult)
            v.drain()
            v.tensor_sub(EQ8[0:NST, cdst:cdst + 1], SQT[0:NST, 0:1], SQT[0:NST, 1:2])
            v.drain()
            v.tensor_scalar(SQT[0:NST, 0:1], EQ8[0:NST, csrc:csrc + 1], LAM[:, 15:16], None, OP.mult)
            v.drain()
            v.tensor_scalar(SQT[0:NST, 1:2], EQ8[0:NST, NSUP + csrc:NSUP + csrc + 1], LAM[:, 14:15], None, OP.mult)
            v.drain()
            v.tensor_add(EQ8[0:NST, NSUP + cdst:NSUP + cdst + 1], SQT[0:NST, 0:1], SQT[0:NST, 1:2])
            v.drain()
        v.tensor_scalar(EQ8[0:NST, 2 * NSUP:3 * NSUP], EQ8[0:NST, NSUP:2 * NSUP], -1.0, None, OP.mult)
        v.drain()
        v.tensor_copy(EQ8[64:128, :], EQ8[0:NST, :])
        return v.drain()
    em(V, eq8_build)

    def a1eq_stage(v):
        v.memset(LAM[:, 14:15], 1.0)
        v.drain()
        v.memset(LAM[:, 15:16], 0.0)
        v.drain()
        for p in range(8):
            i = 7 - p
            v.tensor_scalar(SQT[0:NST, 0:T], A1R[:, :], LAM[:, 14:15], None, OP.mult)
            v.tensor_scalar(SQT[0:NST, 256:256 + T], A1I[:, :], LAM[:, 15:16], None, OP.mult)
            v.drain()
            v.tensor_sub(SQT[0:NST, 512:512 + T], SQT[0:NST, 0:T], SQT[0:NST, 256:256 + T])
            v.drain()
            v.tensor_copy(STG1[:, i * T:(i + 1) * T], SQT[0:NST, 512:512 + T])
            v.drain()
            v.tensor_scalar(SQT[0:NST, 0:T], A1R[:, :], LAM[:, 15:16], None, OP.mult)
            v.tensor_scalar(SQT[0:NST, 256:256 + T], A1I[:, :], LAM[:, 14:15], None, OP.mult)
            v.drain()
            v.tensor_add(SQT[0:NST, 512:512 + T], SQT[0:NST, 0:T], SQT[0:NST, 256:256 + T])
            v.drain()
            v.tensor_copy(STG2[:, i * T:(i + 1) * T], SQT[0:NST, 512:512 + T])
            v.drain()
            if p < 7:
                v.tensor_scalar(SQT[0:NST, 0:1], LAM[:, 14:15], EQC[:, 0:1], None, OP.mult)
                v.drain()
                v.tensor_scalar(SQT[0:NST, 1:2], LAM[:, 15:16], EQC[:, 1:2], None, OP.mult)
                v.drain()
                v.tensor_sub(SQT[0:NST, 2:3], SQT[0:NST, 0:1], SQT[0:NST, 1:2])
                v.drain()
                v.tensor_scalar(SQT[0:NST, 0:1], LAM[:, 14:15], EQC[:, 1:2], None, OP.mult)
                v.drain()
                v.tensor_scalar(SQT[0:NST, 1:2], LAM[:, 15:16], EQC[:, 0:1], None, OP.mult)
                v.drain()
                v.tensor_add(LAM[:, 15:16], SQT[0:NST, 0:1], SQT[0:NST, 1:2])
                v.drain()
                v.tensor_copy(LAM[:, 14:15], SQT[0:NST, 2:3])
                v.drain()
        v.tensor_copy(CSR[:, 0:T], A1R[:, :])
        v.tensor_copy(CSI[:, 0:T], A1I[:, :])
        v.tensor_copy(CSR[:, T:T + Q2], A1R[:, T - Q2:T])
        v.tensor_copy(CSI[:, T:T + Q2], A1I[:, T - Q2:T])
        v.tensor_copy(P1RB[:, :], P1R[:, 0:Q2])
        v.tensor_copy(P1IB[:, :], P1I[:, 0:Q2])
        return v.drain()
    em(V, a1eq_stage, inc=("sVE", 1))
    ve_stage = P.c("sVE")

    if sub <= 3:
        return finish()
    def tr_pack(dst_ap, srcR_ap, srcI_ap, first_waits=()):
        em(TE, lambda t, a=srcR_ap: t.transpose(PBT[:, 0:64], a, IDB[0:64, 0:64]),
           waits=list(first_waits))
        em(TE, lambda t, a=srcI_ap: t.transpose(PBT[:, 64:128], a, IDB[0:64, 0:64]),
           inc=("sTE", 1))
        mt = P.c("sTE")
        em(V, lambda v, d=dst_ap: v.tensor_copy(d, PBT[:, 0:128]),
           waits=[("sTE", mt)], inc=("sVE", 1))
        return P.c("sVE")

    w0 = [("sVE", ve_stage), ("sGP", gp_ident_gp)]
    lastev = None
    for i in range(8):
        w = w0 if lastev is None else [("sVE", lastev)]
        lastev = tr_pack(A1EQ[:, i * 128:(i + 1) * 128],
                         STG1[:, i * T:(i + 1) * T], STG2[:, i * T:(i + 1) * T], w)
    lastev = tr_pack(A1PK[:, :], CSR[:, 0:T], CSI[:, 0:T], [("sVE", lastev)])
    em(TE, lambda t: t.transpose(PBT[0:32, 0:64], CSR[:, T:T + Q2], IDB[0:64, 0:64]),
       waits=[("sVE", lastev)])
    em(TE, lambda t: t.transpose(PBT[0:32, 64:128], CSI[:, T:T + Q2], IDB[0:64, 0:64]),
       inc=("sTE", 1))
    mt = P.c("sTE")
    def a2l_fill(v):
        v.memset(A2L[:, :], 0.0)
        v.drain()
        for i in range(3):
            v.tensor_copy(A2L[32 * i:32 * i + 32, i * 128:i * 128 + 128],
                          PBT[0:32, 0:128])
        return v.drain()
    em(V, a2l_fill, waits=[("sTE", mt)], inc=("sVE", 1))
    ve_a2l = P.c("sVE")

    if sub <= 4:
        return finish()
    for hc in range(8):
        em(TE, lambda t, hc=hc: t.matmul(PS[0][0:64, 0:128],
                                         CR_STG[:, hc * 64:(hc + 1) * 64],
                                         IDF[:], start=True, stop=True),
           waits=([("sVE", ve_a2l)] if hc == 0 else [("sVE", P.c("sVE"))])
                 + [("sX", ld_small)],
           inc=("sTE", 1))
        mt = P.c("sTE")
        em(V, lambda v, hc=hc: v.tensor_copy(SR_[:, hc * 128:(hc + 1) * 128],
                                             PS[0][0:64, 0:128]),
           waits=[("sTE", mt)], inc=("sVE", 1))
    for hc in range(8):
        em(TE, lambda t, hc=hc: t.matmul(PS[0][0:64, 0:128],
                                         CI_STG[:, hc * 64:(hc + 1) * 64],
                                         IDF[:], start=True, stop=True),
           waits=[("sVE", P.c("sVE"))], inc=("sTE", 1))
        mt = P.c("sTE")
        em(V, lambda v, hc=hc: v.tensor_copy(SI_[:, hc * 128:(hc + 1) * 128],
                                             PS[0][0:64, 0:128]),
           waits=[("sTE", mt)], inc=("sVE", 1))

    def cn_build(v):
        v.drain()
        v.tensor_scalar(VTMP[0:NST, :], SR_[:, :], LAM[:, 6:7], None, OP.mult)
        v.tensor_scalar(SQT[0:NST, :], SI_[:, :], LAM[:, 7:8], None, OP.mult)
        v.drain()
        v.tensor_sub(VTMP[0:NST, :], VTMP[0:NST, :], SQT[0:NST, :])
        v.drain()
        v.tensor_copy(CNR[:, :], VTMP[0:NST, :])
        v.drain()
        v.tensor_scalar(VTMP[0:NST, :], SR_[:, :], LAM[:, 7:8], None, OP.mult)
        v.tensor_scalar(SQT[0:NST, :], SI_[:, :], LAM[:, 6:7], None, OP.mult)
        v.drain()
        v.tensor_add(VTMP[0:NST, :], VTMP[0:NST, :], SQT[0:NST, :])
        v.drain()
        v.tensor_copy(CNI[:, :], VTMP[0:NST, :])
        v.drain()
        v.tensor_scalar(VTMP[0:NST, :], VTMP[0:NST, :], -1.0, None, OP.mult)
        v.drain()
        v.tensor_copy(CNINB[:, :], VTMP[0:NST, :])
        v.memset(SINR[:, :], 0.0)
        v.memset(SRS[:, :], 0.0)
        return v.drain()
    em(V, cn_build, inc=("sVE", 1))
    ve_cn = P.c("sVE")

    if sub <= 5:
        return finish()
    for hc in range(8):
        em(TE, lambda t, hc=hc: t.matmul(PS[0][:, hc * 32:(hc + 1) * 32],
                                         CNR[:, hc * 128:(hc + 1) * 128],
                                         P1RB[:, :], start=True, stop=False),
           waits=[("sVE", ve_cn)])
        em(TE, lambda t, hc=hc: t.matmul(PS[0][:, hc * 32:(hc + 1) * 32],
                                         CNINB[:, hc * 128:(hc + 1) * 128],
                                         P1IB[:, :], start=False, stop=True),
           inc=(("sTE", 1) if hc == 7 else None))
    k32_mt = P.c("sTE")
    em(V, lambda v: v.tensor_copy(K32[:, 0:8 * Q2], PS[0][:, 0:256]),
       waits=[("sTE", k32_mt)], inc=("sVE", 1))
    ve_k32 = P.c("sVE")

    if limit <= 0:
        return finish()
    slot_v = [None, None]
    slot_s = [None, None]
    xm = {}
    chb_mm = {}
    hi_mm = {}
    lo_mm = {}
    ob2_mm = {}
    x2_mm = {}
    dx_mm = {}
    rec_j_ms = {}
    sup_read = None
    score_group = 0
    own_lo_ready = {}
    own_hi_ready = {}
    xn2_ready = {}
    chb_ready = {}
    stat_free_s = None
    prev_affine_s = None
    prev_vec_fin = None
    hi_evac = {}
    lo_evac = {}
    score_ms = {}
    sq_free_v = None
    pbt_free = lastev
    pbt_free = P.c("sVE")

    for c in range(128):
        own = c >= NPRE
        j = c - NPRE
        slot = c % 2
        sup = c // 8 if not own else None
        w = []
        if slot_v[slot] is not None:
            w.append(("sVE", slot_v[slot]))
        if slot_s[slot] is not None:
            w.append(("sSC", slot_s[slot]))
        if c < 2:
            w.append(("sX", ld_dgb))
            w.append(("sVE", ve_cn))
        xs = XSTG[:, slot * 1024:(slot + 1) * 1024]
        xsem = "sXa" if slot == 0 else "sXb"
        dma(SY, xs, d_x[c * T:(c + 1) * T, :], waits=w, sem=xsem)
        xm[c] = (xsem, P.c(xsem))

        wv = [xm[c]]
        if stat_free_s is not None:
            wv.append(("sSC", stat_free_s))
        em(V, lambda v, xs=xs: v.reduce_sum(STAT[:, 0:1], xs, axis=mybir.AxisListType.X),
           waits=wv, inc=("sVE", 1))
        m_red = P.c("sVE")
        wsq = [xm[c], ("sVE", m_red)]
        if sq_free_v is not None and own:
            pass
        if sq_free_v is not None:
            wsq.append(("sVE", sq_free_v))
        em(SC, lambda s, xs=xs: s.activation(SQT, xs, AF.Square, accum_out=STAT[:, 1:2]),
           waits=wsq, inc=("sSC", 1))
        m_sq = P.c("sSC")

        def stats1(v):
            v.tensor_scalar(STAT[:, 6:7], STAT[:, 0:1], 1.0 / H, None, OP.mult)
            v.drain()
            v.tensor_scalar(STAT[:, 7:8], STAT[:, 1:2], 1.0 / H, None, OP.mult)
            v.drain()
            v.tensor_mul(STAT[:, 13:14], STAT[:, 6:7], STAT[:, 6:7])
            v.drain()
            v.tensor_sub(STAT[:, 7:8], STAT[:, 7:8], STAT[:, 13:14])
            v.drain()
            v.tensor_scalar(STAT[:, 2:3], STAT[:, 7:8], EPS, None, OP.add)
            v.drain()
            v.reciprocal(STAT[:, 14:15], STAT[:, 2:3])
            v.drain()
            v.tensor_mul(STAT[:, 13:14], STAT[:, 7:8], STAT[:, 14:15])
            v.drain()
            v.tensor_scalar(STAT[:, 3:4], STAT[:, 13:14], EPS, None, OP.add)
            return v.drain()
        em(V, stats1, waits=[("sSC", m_sq)], inc=("sVE", 1))
        m_st1 = P.c("sVE")
        em(SC, lambda s: s.activation(STAT[:, 4:6], STAT[:, 2:4], AF.Sqrt),
           waits=[("sVE", m_st1)], inc=("sSC", 1))
        m_sqrt = P.c("sSC")

        def stats2(v, c=c, own=own):
            v.reciprocal(STAT[:, 8:10], STAT[:, 4:6])
            v.drain()
            v.tensor_mul(STAT[:, 13:15], STAT[:, 2:4], STAT[:, 8:10])
            v.drain()
            v.tensor_add(STAT[:, 4:6], STAT[:, 4:6], STAT[:, 13:15])
            v.drain()
            v.tensor_scalar(STAT[:, 4:6], STAT[:, 4:6], 0.5, None, OP.mult)
            v.drain()
            v.reciprocal(STAT[:, 8:10], STAT[:, 4:6])
            v.drain()
            v.tensor_mul(STAT[:, 10:11], STAT[:, 8:9], STAT[:, 9:10])
            v.drain()
            if not own:
                v.tensor_mul(STAT[:, 10:11], STAT[:, 10:11], MASKB[:, c:c + 1])
                v.drain()
            v.tensor_mul(STAT[:, 11:12], STAT[:, 6:7], STAT[:, 10:11])
            v.drain()
            v.tensor_scalar(STAT[:, 11:12], STAT[:, 11:12], -1.0, None, OP.mult)
            v.drain()
            if own:
                v.tensor_mul(STAT[:, 12:13], STAT[:, 6:7], STAT[:, 8:9])
                v.drain()
                v.tensor_scalar(STAT[:, 12:13], STAT[:, 12:13], -1.0, None, OP.mult)
                return v.drain()
            return v.drain()
        em(V, stats2, waits=[("sSC", m_sqrt), ("sWM", ld_maskb)], inc=("sVE", 1))
        m_st2 = P.c("sVE")
        if DEBUG_TAPS and c == NPRE:
            em(SY, lambda s: s.dma_start(out=d_tap1, in_=STAT[:, :]),
               waits=[("sVE", m_st2)], inc=("sST", 16))

        if not own:
            cb = PRECHB
            wch = [("sVE", m_st2)]
            if c == 0:
                wch.append(("sTE", k32_mt))
            if c >= 1:
                wch.append(("sTE", chb_mm[c - 1]))
            em(SC, lambda s, xs=xs, cb=cb: s.activation(
                cb, xs, AF.Identity, bias=STAT[:, 11:12], scale=STAT[:, 10:11]),
               waits=wch, inc=("sSC", 1))
            chb_ready[c] = P.c("sSC")
            stat_free_s = P.c("sSC")
            wmm = [("sSC", chb_ready[c]), ("sVE", ve_k32), ("sGP", gp_ident_gp)]
            if c % 8 == 0 and sup_read is not None:
                wmm.append(("sVE", sup_read))
            i = c % 8
            first = (i == 0)
            last = (i == 7)
            em(TE, lambda t, i=i, cb=cb, first=first, last=last: t.matmul(
                PS[0][:, :], A1EQ[:, i * 128:(i + 1) * 128], cb[:, 0:512],
                start=first, stop=last, skip_group_check=True), waits=wmm)
            em(TE, lambda t, i=i, cb=cb, first=first, last=last: t.matmul(
                PS[1][:, :], A1EQ[:, i * 128:(i + 1) * 128], cb[:, 512:1024],
                start=first, stop=last, skip_group_check=True))
            em(TE, lambda t, i=i, c=c, first=first, last=last: t.matmul(
                PS[2][:, 0:1], A1EQ[:, i * 128:(i + 1) * 128], MASKB[:, c:c + 1],
                start=first, stop=last, skip_group_check=True),
               inc=("sTE", 1))
            chb_mm[c] = P.c("sTE")
            if last:
                mt = P.c("sTE")
                def sup_acc(v, s=sup):
                    for hh in range(2):
                        tgt_r = SINR[0:64, hh * 512:(hh + 1) * 512]
                        tgt_i = SINR[64:128, hh * 512:(hh + 1) * 512]
                        v.scalar_tensor_tensor(tgt_r, PS[hh][0:64, :],
                                               EQ8[0:NST, s:s + 1],
                                               tgt_r, OP.mult, OP.add)
                        v.scalar_tensor_tensor(tgt_i, PS[hh][0:64, :],
                                               EQ8[64:128, NSUP + s:NSUP + s + 1],
                                               tgt_i, OP.mult, OP.add)
                        v.drain()
                        v.scalar_tensor_tensor(tgt_r, PS[hh][64:128, :],
                                               EQ8[0:NST, 2 * NSUP + s:2 * NSUP + s + 1],
                                               tgt_r, OP.mult, OP.add)
                        v.scalar_tensor_tensor(tgt_i, PS[hh][64:128, :],
                                               EQ8[64:128, s:s + 1],
                                               tgt_i, OP.mult, OP.add)
                        v.drain()
                    v.scalar_tensor_tensor(SRS[0:64, 0:1], PS[2][0:64, 0:1],
                                           EQ8[0:NST, s:s + 1],
                                           SRS[0:64, 0:1], OP.mult, OP.add)
                    v.drain()
                    v.scalar_tensor_tensor(SRS[0:64, 0:1], PS[2][64:128, 0:1],
                                           EQ8[0:NST, 2 * NSUP + s:2 * NSUP + s + 1],
                                           SRS[0:64, 0:1], OP.mult, OP.add)
                    v.drain()
                    v.scalar_tensor_tensor(SRS[64:128, 0:1], PS[2][0:64, 0:1],
                                           EQ8[64:128, NSUP + s:NSUP + s + 1],
                                           SRS[64:128, 0:1], OP.mult, OP.add)
                    v.drain()
                    v.scalar_tensor_tensor(SRS[64:128, 0:1], PS[2][64:128, 0:1],
                                           EQ8[64:128, s:s + 1], SRS[64:128, 0:1],
                                           OP.mult, OP.add)
                    return v.drain()
                em(V, sup_acc, waits=[("sTE", mt), ("sVE", ve_cn)], inc=("sVE", 1))
                sup_read = P.c("sVE")
            slot_v[slot] = m_st2
            slot_s[slot] = P.c("sSC")
        else:
            ob = OWNB[:, (j % 2) * H:(j % 2) * H + H]
            cb = CHB[:, :]
            wsc = [("sVE", m_st2)]
            if j == 0:
                wsc.append(("sTE", chb_mm[NPRE - 1]))
            if j >= 2:
                wsc.append(("sTE", hi_mm[j - 2]))
            em(SC, lambda s, xs=xs: (s.activation(SQT, xs, AF.Identity,
                                                  bias=STAT[:, 12:13], scale=STAT[:, 8:9]),
                                     s.drain())[0], waits=wsc)
            em(SC, lambda s, ob=ob: s.activation(ob, SQT, AF.Identity),
               inc=("sSC", 1))
            own_hi_ready[j] = P.c("sSC")
            wob2 = []
            if j >= 1:
                wob2.append(("sTE", ob2_mm[j - 1]))
            em(SC, lambda s: s.activation(OB2[:, :], SQT, AF.Identity,
                                          scale=STAT[:, 9:10]),
               waits=wob2, inc=("sSC", 1))
            ob2_ready = P.c("sSC")
            stat_free_s = ob2_ready
            wlo = [("sSC", own_hi_ready[j])]
            if j == 0:
                wlo.append(("sTE", chb_mm[NPRE - 1]))
            if j >= 1:
                wlo.append(("sTE", lo_mm[j - 1]))
            em(V, lambda v, cb=cb, ob=ob: (v.tensor_sub(cb, SQT, ob), v.drain())[0],
               waits=wlo, inc=("sVE", 1))
            own_lo_ready[j] = P.c("sVE")
            sq_free_v = P.c("sVE")
            wtr = [("sSC", own_hi_ready[j]), ("sVE", pbt_free)]
            for hc in range(8):
                em(TE, lambda t, hc=hc, ob=ob: t.transpose(
                    PBT[:, hc * 128:(hc + 1) * 128], ob[:, hc * 128:(hc + 1) * 128],
                    IDB[:]), waits=(wtr if hc == 0 else []),
                   inc=(("sTE", 1) if hc == 7 else None))
            hi_mm[j] = P.c("sTE")
            em(V, lambda v, j=j: v.tensor_copy(
                XNFM[:, :].rearrange("p (hc t) -> p hc t", t=LC)[:, :, j * 128:(j + 1) * 128],
                PBT[:, 0:1024].rearrange("p (hc t) -> p hc t", t=128)),
               waits=[("sTE", hi_mm[j])], inc=("sVE", 1))
            hi_evac[j] = P.c("sVE")
            wtr2 = [("sVE", own_lo_ready[j]), ("sVE", hi_evac[j])]
            if j >= 1:
                wtr2.append(("sTE", score_ms[j - 1]))
            for hc in range(8):
                em(TE, lambda t, hc=hc, cb=cb: t.transpose(
                    PBT[:, hc * 128:(hc + 1) * 128], cb[:, hc * 128:(hc + 1) * 128],
                    IDB[:]), waits=(wtr2 if hc == 0 else []),
                   inc=(("sTE", 1) if hc == 7 else None))
            lo_mm[j] = P.c("sTE")
            em(V, lambda v: v.tensor_copy(XNLOF[:, :], PBT[:, 0:1024]),
               waits=[("sTE", lo_mm[j])], inc=("sVE", 1))
            pbt_free = P.c("sVE")
            wtr3 = [("sSC", ob2_ready), ("sVE", pbt_free)]
            for hc in range(8):
                em(TE, lambda t, hc=hc: t.transpose(
                    PBT[:, hc * 128:(hc + 1) * 128], OB2[:, hc * 128:(hc + 1) * 128],
                    IDB[:]), waits=(wtr3 if hc == 0 else []),
                   inc=(("sTE", 1) if hc == 7 else None))
            ob2_mm[j] = P.c("sTE")
            wxf = [("sTE", ob2_mm[j])]
            if j >= 1:
                wxf.append(("sTE", x2_mm[j - 1]))
            for hc in range(8):
                em(SC, lambda s, hc=hc: s.activation(
                    XF2[:, hc * 128:(hc + 1) * 128], PBT[:, hc * 128:(hc + 1) * 128],
                    AF.Identity, bias=BCOL[:, hc:hc + 1], scale=GCOL[:, hc:hc + 1]),
                   waits=(wxf if hc == 0 else []),
                   inc=(("sSC", 1) if hc == 7 else None))
            xf2_sc = P.c("sSC")
            wdx = [("sSC", xf2_sc)]
            if j >= 1:
                wdx.append(("sTE", dx_mm[j - 1]))
            def dxf(v):
                for hc in range(8):
                    v.tensor_scalar(DXF2[:, hc * 128:(hc + 1) * 128],
                                    XF2[:, hc * 128:(hc + 1) * 128],
                                    DROW[:, hc:hc + 1], None, OP.mult)
                return v.drain()
            em(V, dxf, waits=wdx, inc=("sVE", 1))
            dxf_ve = P.c("sVE")
            wx2t = [("sSC", xf2_sc), ("sVE", pbt_free)]
            for hc in range(8):
                em(TE, lambda t, hc=hc: t.transpose(
                    PBT[:, hc * 128:(hc + 1) * 128], XF2[:, hc * 128:(hc + 1) * 128],
                    IDB[:]), waits=(wx2t if hc == 0 else []),
                   inc=(("sTE", 1) if hc == 7 else None))
            x2_mm[j] = P.c("sTE")
            em(V, lambda v, j=j: v.tensor_copy(
                XN2TM[:, j * 1024:(j + 1) * 1024], PBT[:, 0:1024]),
               waits=[("sTE", x2_mm[j])], inc=("sVE", 1))
            xn2_ready[j] = P.c("sVE")
            wdxt = [("sVE", dxf_ve), ("sVE", xn2_ready[j])]
            for hc in range(8):
                em(TE, lambda t, hc=hc: t.transpose(
                    PBT[:, hc * 128:(hc + 1) * 128], DXF2[:, hc * 128:(hc + 1) * 128],
                    IDB[:]), waits=(wdxt if hc == 0 else []),
                   inc=(("sTE", 1) if hc == 7 else None))
            dx_mm[j] = P.c("sTE")
            em(V, lambda v, j=j, xs=xs: (v.tensor_add(
                ACC[:, j * 1024:(j + 1) * 1024], PBT[:, 0:1024], xs), v.drain())[0],
               waits=[("sTE", dx_mm[j])], inc=("sVE", 1))
            pbt_free = P.c("sVE")
            slot_v[slot] = P.c("sVE")
            slot_s[slot] = own_hi_ready[j]
            wsc2 = [("sVE", hi_evac[j]), ("sVE", P.c("sVE") if j == 0 else hi_evac[j])]
            wsc2 = [("sVE", pbt_free)]
            if j == 0 and sup_read is not None:
                wsc2.append(("sVE", sup_read))
            for ph in range(3):
                for hc in range(8):
                    if ph == 0:
                        lhsT = XNFM[:, hc * LC + j * 128: hc * LC + (j + 1) * 128]
                        rhs = GATEH[:, hc * E:(hc + 1) * E]
                    elif ph == 1:
                        lhsT = XNFM[:, hc * LC + j * 128: hc * LC + (j + 1) * 128]
                        rhs = GATEL[:, hc * E:(hc + 1) * E]
                    else:
                        lhsT = XNLOF[:, hc * 128:(hc + 1) * 128]
                        rhs = GATEH[:, hc * E:(hc + 1) * E]
                    first = (ph == 0 and hc == 0)
                    last = (ph == 2 and hc == 7)
                    em(TE, lambda t, lhsT=lhsT, rhs=rhs, j=j, first=first, last=last:
                       t.matmul(PS[2][:, 128 + j * 8:136 + j * 8], lhsT, rhs,
                                start=first, stop=last, skip_group_check=True),
                       waits=(wsc2 if first else []),
                       inc=(("sTE", 1) if last else None))
            score_ms[j] = P.c("sTE")

    if limit <= 1:
        return finish()
    em(V, lambda v: (v.tensor_copy(SCORES[:, :], PS[2][:, 128:256]), v.drain())[0],
       waits=[("sTE", score_ms[NOWN - 1]), ("sSC", P.c("sSC"))], inc=("sVE", 1))
    scores_ev = P.c("sVE")
    dma(SY, GBt, d_gamma.unsqueeze(0).partition_broadcast(128),
        waits=[("sVE", scores_ev), ("sSC", P.c("sSC"))])
    dma(SY, BBt, d_beta.unsqueeze(0).partition_broadcast(128))
    gb_ld = P.c("sX")

    def sin_fix(v):
        v.tensor_scalar(VTMP[0:64, :], BBt[0:64, :], SRS[0:64, 0:1], None, OP.mult)
        v.tensor_scalar(VTMP[64:128, :], BBt[64:128, :], SRS[64:128, 0:1], None, OP.mult)
        v.tensor_mul(SINR[:, :], SINR[:, :], GBt)
        v.drain()
        v.tensor_add(SINR[:, :], SINR[:, :], VTMP[:, :])
        return v.drain()
    em(V, sin_fix, waits=[("sVE", sup_read), ("sX", gb_ld)], inc=("sVE", 1))
    sin_done = P.c("sVE")

    def r3(ap):
        return ap.rearrange("p (g e) -> p g e", e=E)
    def routing1(v):
        s3 = r3(SCORES[:, :])
        m1v = RSM[:, 0:16].rearrange("p (g o) -> p g o", o=1)
        v.reduce_max(m1v, s3, axis=mybir.AxisListType.X)
        v.drain()
        m1b = m1v.broadcast_to([128, 16, E])
        v.tensor_scalar(RSC[:, 0:128], SCORES[:, :], -1.0, None, OP.mult)
        v.drain()
        v.tensor_max(RSC[:, 0:128], RSC[:, 0:128], SCORES[:, :])
        v.drain()
        v.tensor_tensor(r3(RSC[:, 128:256]), r3(RSC[:, 0:128]), m1b, OP.max)
        v.drain()
        v.tensor_scalar(RSC[:, 128:256], RSC[:, 128:256], -JIT2, None, OP.mult)
        v.drain()
        v.tensor_tensor(r3(RSC[:, 256:384]), s3, m1b, OP.subtract)
        v.drain()
        v.tensor_tensor(r3(RSC[:, 384:512]), r3(RSC[:, 256:384]), r3(RSC[:, 128:256]), OP.is_ge)
        v.drain()
        v.tensor_tensor(r3(RSC[:, 640:768]), s3, m1b, OP.is_equal)
        return v.drain()
    em(V, routing1, waits=[("sVE", scores_ev)], inc=("sVE", 1))
    em(SC, lambda s: s.activation(RSC[:, 512:640], RSC[:, 256:384], AF.Exp),
       waits=[("sVE", P.c("sVE"))], inc=("sSC", 1))
    m_e1 = P.c("sSC")
    def routing2(v):
        s3 = r3(SCORES[:, :])
        v.tensor_mul(RSC[:, 512:640], RSC[:, 512:640], RSC[:, 384:512])
        v.drain()
        z1v = RSM[:, 16:32].rearrange("p (g o) -> p g o", o=1)
        v.reduce_sum(z1v, r3(RSC[:, 512:640]), axis=mybir.AxisListType.X)
        v.drain()
        v.reciprocal(RSM[:, 32:48], RSM[:, 16:32])
        v.drain()
        v.scalar_tensor_tensor(RSC[:, 768:896], RSC[:, 640:768], -1.0e5,
                               SCORES[:, :], OP.mult, OP.add)
        v.drain()
        m2v = RSM[:, 48:64].rearrange("p (g o) -> p g o", o=1)
        v.reduce_max(m2v, r3(RSC[:, 768:896]), axis=mybir.AxisListType.X)
        v.drain()
        m2b = m2v.broadcast_to([128, 16, E])
        v.tensor_tensor(r3(RSC[:, 128:256]), r3(RSC[:, 0:128]), m2b, OP.max)
        v.drain()
        v.tensor_scalar(RSC[:, 128:256], RSC[:, 128:256], -JIT2, None, OP.mult)
        v.drain()
        v.tensor_tensor(r3(RSC[:, 256:384]), s3, m2b, OP.subtract)
        v.drain()
        v.tensor_tensor(r3(RSC[:, 384:512]), r3(RSC[:, 256:384]), r3(RSC[:, 128:256]), OP.is_ge)
        v.drain()
        v.tensor_scalar(RSC[:, 896:1024], RSC[:, 640:768], -1.0, 1.0, OP.mult, OP.add)
        v.drain()
        v.tensor_mul(RSC[:, 384:512], RSC[:, 384:512], RSC[:, 896:1024])
        v.drain()
        v.tensor_tensor(r3(RSC[:, 896:1024]), r3(RSC[:, 768:896]), m2b, OP.is_equal)
        return v.drain()
    em(V, routing2, waits=[("sSC", m_e1)], inc=("sVE", 1))
    em(SC, lambda s: s.activation(RSC[:, 512:640], RSC[:, 256:384], AF.Exp),
       waits=[("sVE", P.c("sVE"))], inc=("sSC", 1))
    m_e2 = P.c("sSC")
    def routing3(v):
        v.tensor_mul(RSC[:, 512:640], RSC[:, 512:640], RSC[:, 384:512])
        v.drain()
        z2v = RSM[:, 64:80].rearrange("p (g o) -> p g o", o=1)
        v.reduce_sum(z2v, r3(RSC[:, 512:640]), axis=mybir.AxisListType.X)
        v.drain()
        v.reciprocal(RSM[:, 80:96], RSM[:, 64:80])
        v.drain()
        m1b = RSM[:, 32:48].rearrange("p (g o) -> p g o", o=1).broadcast_to([128, 16, E])
        m2b = RSM[:, 80:96].rearrange("p (g o) -> p g o", o=1).broadcast_to([128, 16, E])
        v.tensor_tensor(r3(RSC[:, 0:128]), r3(RSC[:, 640:768]), m1b, OP.mult)
        v.drain()
        v.tensor_tensor(r3(RSC[:, 128:256]), r3(RSC[:, 896:1024]), m2b, OP.mult)
        v.drain()
        v.tensor_add(WTOKSB[:, :], RSC[:, 0:128], RSC[:, 128:256])
        return v.drain()
    em(V, routing3, waits=[("sSC", m_e2)], inc=("sVE", 1))
    wtok_done = P.c("sVE")
    em(V, lambda v: (v.tensor_copy(WTOKD, WTOKSB[:, :]), v.drain())[0],
       waits=[("sVE", wtok_done)], inc=("sVE", 1))
    wtok_stash = P.c("sVE")
    em(V, lambda v: (v.tensor_copy(SR_[:, :], SINR[0:64, :]),
                     v.tensor_copy(SI_[:, :], SINR[64:128, :]), v.drain())[-1],
       waits=[("sVE", wtok_stash)], inc=("sVE", 1))
    sin_done = P.c("sVE")
    if DEBUG_TAPS:
        em(SY, lambda s: s.dma_start(out=d_tap4, in_=WTOKSB[:, :]),
           waits=[("sVE", wtok_done)], inc=("sST", 16))
        em(SY, lambda s: s.dma_start(out=d_tap5, in_=SCORES[:, :]),
           waits=[("sVE", wtok_done)], inc=("sST", 16))

    x1_ms = {}
    cs_ms = {}
    last_rec = sin_done
    last_cs_consumed = score_ms[NOWN - 1]
    last_y_ev = ve_k32
    last_cx_consumed = None
    ydb_ms = {}
    ydb_consumed = {}
    pbt_c_used = pbt_free
    X2C = POOLB[:, 24496:26544]

    for jg in range(8):
        for rnd in range(2):
            wtr = [("sVE", pbt_c_used)]
            for q in range(4):
                hc = rnd * 4 + q
                for jj in range(2):
                    em(TE, lambda t, jg=jg, jj=jj, hc=hc, q=q: t.transpose(
                        PBT[:, q * 256 + jj * 128:q * 256 + (jj + 1) * 128],
                        XN2TM[:, (2 * jg + jj) * 1024 + hc * 128:
                              (2 * jg + jj) * 1024 + (hc + 1) * 128],
                        IDB[:]), waits=(wtr if (q == 0 and jj == 0) else []),
                       inc=(("sTE", 1) if (q == 3 and jj == 1) else None))
            x2t_mt = P.c("sTE")
            em(V, lambda v, rnd=rnd: (v.tensor_copy(
                X2C[:, rnd * 1024:(rnd + 1) * 1024], PBT[:, 0:1024]),
                v.drain())[0], waits=[("sTE", x2t_mt)], inc=("sVE", 1))
            pbt_c_used = P.c("sVE")
        for hc in range(8):
            wv = []
            if jg == 0 and hc == 0:
                wv.append(("sVE", sin_done))
            if jg > 0 and hc == 0:
                wv.append(("sTE", ydb_consumed[jg - 1]))
            def diag(v, hc=hc):
                src_ = X2C[:, hc * 256:(hc + 1) * 256]
                dst = YD32[:, hc * 256:(hc + 1) * 256]
                s4 = src_.rearrange("p (j i k) -> p j i k", j=2, k=Q2)
                d4 = dst.rearrange("p (j i k) -> p j i k", j=2, k=Q2)
                v.tensor_scalar(d4, s4, K32[:, hc * Q2:hc * Q2 + 1], None, OP.mult)
                v.drain()
                for d in range(1, Q2):
                    v.scalar_tensor_tensor(
                        d4[:, :, :, d:Q2], s4[:, :, :, 0:Q2 - d],
                        K32[:, hc * Q2 + d:hc * Q2 + d + 1],
                        d4[:, :, :, d:Q2], OP.mult, OP.add)
                    v.drain()
                return v.drain()
            em(V, diag, waits=wv, inc=("sVE", 1))
        ydb_ms[jg] = P.c("sVE")

        for jj in range(2):
            j = 2 * jg + jj
            par = j % 2
            psx1 = (PS[3], PS[4]) if par == 0 else (PS[5], PS[6])
            wx1 = [("sVE", xn2_ready[j])]
            if j >= 2:
                wx1.append(("sVE", rec_j_ms[j - 2]))
            em(TE, lambda t, j=j, psx1=psx1: t.matmul(
                psx1[0][:, :], A1PK[:, :], XN2TM[:, j * 1024:j * 1024 + 512],
                start=True, stop=True, skip_group_check=True), waits=wx1)
            em(TE, lambda t, j=j, psx1=psx1: t.matmul(
                psx1[1][:, :], A1PK[:, :], XN2TM[:, j * 1024 + 512:(j + 1) * 1024],
                start=True, stop=True, skip_group_check=True), inc=("sTE", 1))
            x1_ms[j] = P.c("sTE")

            wcs = [("sVE", last_rec), ("sTE", last_cs_consumed)]
            def cs_ops(v):
                v.tensor_mul(VTMP[0:NST, :], SR_[:, :], CNR[:, :])
                v.tensor_mul(SQT[0:NST, :], SI_[:, :], CNI[:, :])
                v.drain()
                v.tensor_sub(CSR[:, :], VTMP[0:NST, :], SQT[0:NST, :])
                v.drain()
                v.tensor_mul(VTMP[0:NST, :], SR_[:, :], CNI[:, :])
                v.tensor_mul(SQT[0:NST, :], SI_[:, :], CNR[:, :])
                v.drain()
                v.tensor_add(CSI[:, :], VTMP[0:NST, :], SQT[0:NST, :])
                return v.drain()
            em(V, cs_ops, waits=wcs, inc=("sVE", 1))
            cs_ms[j] = P.c("sVE")

            def rec_ops(v, psx1=psx1):
                v.tensor_scalar(VTMP[0:64, :], SR_[:, :], EQC[:, 0:1], None, OP.mult)
                v.tensor_scalar(SQT[0:64, :], SI_[:, :], EQC[:, 0:1], None, OP.mult)
                v.drain()
                v.scalar_tensor_tensor(VTMP[0:64, :], SI_[:, :], EQC[:, 2:3],
                                       VTMP[0:64, :], OP.mult, OP.add)
                v.scalar_tensor_tensor(SQT[0:64, :], SR_[:, :], EQC[:, 1:2],
                                       SQT[0:64, :], OP.mult, OP.add)
                v.drain()
                v.tensor_add(SR_[:, 0:512], VTMP[0:64, 0:512], psx1[0][0:64, :])
                v.tensor_add(SR_[:, 512:1024], VTMP[0:64, 512:1024], psx1[1][0:64, :])
                v.tensor_add(SI_[:, 0:512], SQT[0:64, 0:512], psx1[0][64:128, :])
                v.tensor_add(SI_[:, 512:1024], SQT[0:64, 512:1024], psx1[1][64:128, :])
                return v.drain()
            em(V, rec_ops, waits=[("sTE", x1_ms[j]), ("sVE", cs_ms[j])], inc=("sVE", 1))
            rec_j_ms[j] = P.c("sVE")
            last_rec = rec_j_ms[j]

            cx_ms = []
            for ip in range(3):
                wx2 = [("sVE", xn2_ready[j])]
                if ip == 0 and j == 0:
                    wx2.append(("sVE", sup_read))
                if ip > 0:
                    wx2.append(("sVE", cx_ms[ip - 1]))
                elif last_cx_consumed is not None:
                    wx2.append(("sTE", last_cx_consumed))
                em(TE, lambda t, j=j, ip=ip: t.matmul(
                    PS[0][:, :], A2L[:, ip * 128:(ip + 1) * 128],
                    XN2TM[:, j * 1024:j * 1024 + 512],
                    start=True, stop=True, skip_group_check=True), waits=wx2)
                em(TE, lambda t, j=j, ip=ip: t.matmul(
                    PS[1][:, :], A2L[:, ip * 128:(ip + 1) * 128],
                    XN2TM[:, j * 1024 + 512:(j + 1) * 1024],
                    start=True, stop=True, skip_group_check=True), inc=("sTE", 1))
                mt = P.c("sTE")
                def cx_ops(v, ip=ip):
                    for hh in range(2):
                        ph = PS[hh]
                        v.tensor_mul(VTMP[0:64, hh * 512:(hh + 1) * 512], ph[0:64, :],
                                     CNR[:, hh * 512:(hh + 1) * 512])
                        v.tensor_mul(SQT[0:64, hh * 512:(hh + 1) * 512], ph[64:128, :],
                                     CNI[:, hh * 512:(hh + 1) * 512])
                    v.drain()
                    v.tensor_sub(CX2R[:, ip * H:(ip + 1) * H], VTMP[0:64, :], SQT[0:64, :])
                    v.drain()
                    for hh in range(2):
                        ph = PS[hh]
                        v.tensor_mul(VTMP[0:64, hh * 512:(hh + 1) * 512], ph[0:64, :],
                                     CNI[:, hh * 512:(hh + 1) * 512])
                        v.tensor_mul(SQT[0:64, hh * 512:(hh + 1) * 512], ph[64:128, :],
                                     CNR[:, hh * 512:(hh + 1) * 512])
                    v.drain()
                    v.tensor_add(CX2I[:, ip * H:(ip + 1) * H],
                                 VTMP[0:64, :], SQT[0:64, :])
                    return v.drain()
                em(V, cx_ops, waits=[("sTE", mt)], inc=("sVE", 1))
                cx_ms.append(P.c("sVE"))

            for hh in range(2):
                wy = [("sVE", cs_ms[j]), ("sVE", cx_ms[2]), ("sVE", ydb_ms[jg]),
                      ("sVE", last_y_ev)]
                em(TE, lambda t, hh=hh: t.matmul(
                    PS[2][:, :], G1RB[:, :], CSR[:, hh * 512:(hh + 1) * 512],
                    start=True, stop=False, skip_group_check=True), waits=wy)
                em(TE, lambda t, hh=hh: t.matmul(
                    PS[2][:, :], G1INB[:, :], CSI[:, hh * 512:(hh + 1) * 512],
                    start=False, stop=False, skip_group_check=True))
                for m in (1, 2, 3):
                    em(TE, lambda t, hh=hh, m=m: t.matmul(
                        PS[2][:, :], G1S[:, (m - 1) * T:m * T],
                        CX2R[:, (m - 1) * H + hh * 512:(m - 1) * H + (hh + 1) * 512],
                        start=False, stop=False, skip_group_check=True))
                    em(TE, lambda t, hh=hh, m=m: t.matmul(
                        PS[2][:, :], G1S[:, (2 + m) * T:(3 + m) * T],
                        CX2I[:, (m - 1) * H + hh * 512:(m - 1) * H + (hh + 1) * 512],
                        start=False, stop=False, skip_group_check=True))
                for q in range(4):
                    hc = hh * 4 + q
                    em(TE, lambda t, jj=jj, hc=hc, q=q: t.matmul(
                        PS[2][:, q * 128:(q + 1) * 128],
                        YD32[:, hc * 256 + jj * 128:hc * 256 + (jj + 1) * 128],
                        IDF[:], start=False, stop=(q == 3), skip_group_check=True),
                       inc=(("sTE", 1) if q == 3 else None))
                mt = P.c("sTE")
                em(V, lambda v, j=j, hh=hh: (v.tensor_add(
                    ACC[:, j * H + hh * 512:j * H + (hh + 1) * 512],
                    ACC[:, j * H + hh * 512:j * H + (hh + 1) * 512], PS[2][:, :]),
                    v.drain())[0],
                   waits=[("sTE", mt)], inc=("sVE", 1))
                last_y_ev = P.c("sVE")
            last_cs_consumed = mt
            last_cx_consumed = mt
        ydb_consumed[jg] = P.c("sTE")

    if limit <= 3:
        return finish()
    fbseq = [(e, fb) for e in range(min(E, dlim)) for fb in range(8)]
    wm13 = {}
    wm2 = {}
    slot_last_mm = {}

    def emit_w13(idx):
        e, fb = fbseq[idx]
        slot = idx % 3
        if idx < 3:
            w = [("sTE", P.c("sTE")), ("sVE", P.c("sVE")), ("sSC", P.c("sSC"))]
        else:
            w = [("sTE", slot_last_mm[idx - 3])]
        wsem = f"sW{slot}"
        dma(GP, W1B[:, slot * 4096:(slot + 1) * 4096].rearrange(
            "p (hc f) -> p hc f", f=512),
            d_w1[e].rearrange("(hc p) (fb f) -> fb p hc f", p=128, f=512)[fb],
            waits=w, sem=wsem)
        dma(GP, W3B[:, slot * 4096:(slot + 1) * 4096].rearrange(
            "p (hc f) -> p hc f", f=512),
            d_w3[e].rearrange("(hc p) (fb f) -> fb p hc f", p=128, f=512)[fb],
            sem=wsem)
        wm13[idx] = (wsem, P.c(wsem))

    def emit_w2(idx):
        e, fb = fbseq[idx]
        slot = idx % 2
        if idx < 2:
            w = [("sTE", P.c("sTE")), ("sVE", P.c("sVE")), ("sSC", P.c("sSC"))]
        else:
            w = [("sTE", slot_last_mm[idx - 2])]
        wsem = "sV2a" if slot == 0 else "sV2b"
        dma(GP, W2B[:, slot * 4096:(slot + 1) * 4096].rearrange(
            "p (fc h) -> p fc h", h=1024),
            d_w2[e].rearrange("(fb fc p) h -> fb p fc h", p=128, fc=4)[fb],
            waits=w, sem=wsem)
        wm2[idx] = (wsem, P.c(wsem))

    par_free_s = [None, None]
    par_free_v = [None, None]
    gsb_free = [None, None]
    moe_start_ve = P.c("sVE")
    emit_w13(0)
    emit_w13(1)
    emit_w13(2)
    emit_w2(0)
    emit_w2(1)
    wrot_free = [moe_start_ve, moe_start_ve, moe_start_ve]
    wrot_ix = 0
    gv_ms = {}
    acc_final = None

    for e in range(min(E, dlim)):
        for p in range(4):
            ifb0, ifb1 = e * 8 + 2 * p, e * 8 + 2 * p + 1
            s0, s1 = ifb0 % 3, ifb1 % 3
            for tb in range(4):
                gbuf = GSB[:, (tb % 2) * 4096:(tb % 2) * 4096 + 4096]
                for half, (ifb, sl) in enumerate(((ifb0, s0), (ifb1, s1))):
                    for fsub in range(4):
                        par = fsub % 2
                        ph1 = PS[0] if par == 0 else PS[2]
                        ph3 = PS[1] if par == 0 else PS[3]
                        wg = [wm13[ifb], ("sVE", moe_start_ve)]
                        if par_free_s[par] is not None:
                            wg.append(("sSC", par_free_s[par]))
                        if par_free_v[par] is not None:
                            wg.append(("sVE", par_free_v[par]))
                        for hc in range(8):
                            em(TE, lambda t, sl=sl, hc=hc, fsub=fsub, tb=tb, ph1=ph1:
                               t.matmul(ph1[:, :],
                                        W1B[:, sl * 4096 + hc * 512 + fsub * 128:
                                            sl * 4096 + hc * 512 + (fsub + 1) * 128],
                                        XNFM[:, hc * LC + tb * 512:hc * LC + (tb + 1) * 512],
                                        start=(hc == 0), stop=(hc == 7),
                                        skip_group_check=True),
                               waits=(wg if hc == 0 else []))
                        for hc in range(8):
                            em(TE, lambda t, sl=sl, hc=hc, fsub=fsub, tb=tb, ph3=ph3:
                               t.matmul(ph3[:, :],
                                        W3B[:, sl * 4096 + hc * 512 + fsub * 128:
                                            sl * 4096 + hc * 512 + (fsub + 1) * 128],
                                        XNFM[:, hc * LC + tb * 512:hc * LC + (tb + 1) * 512],
                                        start=(hc == 0), stop=(hc == 7),
                                        skip_group_check=True),
                               inc=(("sTE", 1) if hc == 7 else None))
                        mt = P.c("sTE")
                        em(SC, lambda s, par=par, ph1=ph1: (s.activation(
                            SILT[:, par * 512:(par + 1) * 512], ph1[:, :], AF.Silu),
                            s.drain())[0],
                           waits=[("sTE", mt)], inc=("sSC", 1))
                        par_free_s[par] = P.c("sSC")
                        wgv = [("sSC", par_free_s[par]), ("sVE", moe_start_ve)]
                        if gsb_free[tb % 2] is not None:
                            wgv.append(("sTE", gsb_free[tb % 2]))
                        gcol = (half * 4 + fsub) * 512
                        def gmul(v, par=par, ph3=ph3, gbuf=gbuf, gcol=gcol):
                            v.tensor_mul(gbuf[:, gcol:gcol + 512],
                                         SILT[:, par * 512:(par + 1) * 512],
                                         ph3[:, :])
                            return v.drain()
                        em(V, gmul, waits=wgv, inc=("sVE", 1))
                        par_free_v[par] = P.c("sVE")
                        gv_ms[(e, 2 * p + half, tb, fsub)] = P.c("sVE")
                for tw in range(2):
                    for hhh in range(2):
                        for tt2 in range(2):
                            tt = tw * 2 + tt2
                            rot = 4 + (wrot_ix % 3)
                            wrot_ix += 1
                            ww = [("sVE", gv_ms[(e, 2 * p + 1, tb, 3)]),
                                  ("sVE", wrot_free[rot - 4]),
                                  wm2[ifb0], wm2[ifb1]]
                            for fc in range(8):
                                half, fsub = fc // 4, fc % 4
                                sl = half
                                em(TE, lambda t, sl=sl, fc=fc, half=half, fsub=fsub,
                                   tt=tt, hhh=hhh, rot=rot, gbuf=gbuf:
                                   t.matmul(PS[rot][:, :],
                                            gbuf[:, (half * 4 + fsub) * 512 + tt * 128:
                                                 (half * 4 + fsub) * 512 + (tt + 1) * 128],
                                            W2B[:, sl * 4096 + fsub * 1024 + hhh * 512:
                                                sl * 4096 + fsub * 1024 + (hhh + 1) * 512],
                                            start=(fc == 0), stop=(fc == 7),
                                            skip_group_check=True),
                                   waits=(ww if fc == 0 else []),
                                   inc=(("sTE", 1) if fc == 7 else None))
                            mt = P.c("sTE")
                            slot_last_mm[ifb0] = mt
                            slot_last_mm[ifb1] = mt
                            col = (tb * 4 + tt) * H + hhh * 512
                            wcol = (tb * 4 + tt) * E + e
                            em(V, lambda v, col=col, rot=rot, wcol=wcol: (
                                v.scalar_tensor_tensor(
                                    ACC[:, col:col + 512], PS[rot][:, :],
                                    WTOKD[:, wcol:wcol + 1],
                                    ACC[:, col:col + 512], OP.mult, OP.add),
                                v.drain())[0],
                               waits=[("sTE", mt)], inc=("sVE", 1))
                            wrot_free[rot - 4] = P.c("sVE")
                            acc_final = P.c("sVE")
                gsb_free[tb % 2] = P.c("sTE")
            for nxt in (ifb1 + 2, ifb1 + 3):
                if nxt < len(fbseq) and nxt not in wm13:
                    emit_w13(nxt)
            for nxt in (ifb1 + 1, ifb1 + 2):
                if nxt < len(fbseq) and nxt not in wm2:
                    emit_w2(nxt)

    em(SY, lambda s: s.dma_start(out=d_out.rearrange("(j p) h -> p j h", p=128),
                                 in_=ACC[:, :].rearrange("p (j h) -> p j h", h=H)),
       waits=[("sVE", acc_final)], inc=("sST", 16))
    em(SY, lambda s: s.wait_ge(sems["sST"], P.c("sST")))

    with nc.Block() as block:
        def mk(engname):
            def run(eng):
                for w, fn, inc in P.q[engname]:
                    for s, v in w:
                        eng.wait_ge(sems[s], v)
                    ins = fn(eng)
                    if inc is not None:
                        assert ins is not None, f"{engname} op needs inst for inc"
                        ins.then_inc(sems[inc[0]], inc[1])
            return run
        block.sync(mk("sync"))
        block.gpsimd(mk("gpsimd"))
        block.scalar(mk("scalar"))
        block.vector(mk("vector"))
        block.tensor(mk("tensor"))

    return nc


_NC_CACHE = [None]


def kernel(L0_attn_x, L0_attn_gamma, L0_attn_beta, L0_attn_LR, L0_attn_LI,
           L0_attn_CR, L0_attn_CI, L0_attn_D, L0_ffn_gate_w, L0_ffn_w1,
           L0_ffn_w2, L0_ffn_w3):
    x = np.ascontiguousarray(np.asarray(L0_attn_x, dtype=np.float32)[0])
    gate_wT = np.ascontiguousarray(np.asarray(L0_ffn_gate_w, np.float32).T)
    w1T = np.ascontiguousarray(np.asarray(L0_ffn_w1, np.float32).transpose(0, 2, 1))
    w2T = np.ascontiguousarray(np.asarray(L0_ffn_w2, np.float32).transpose(0, 2, 1))
    w3T = np.ascontiguousarray(np.asarray(L0_ffn_w3, np.float32).transpose(0, 2, 1))
    com = {
        "gate_wT": gate_wT, "w1T": w1T, "w2T": w2T, "w3T": w3T,
        "CR": np.ascontiguousarray(np.asarray(L0_attn_CR, np.float32)),
        "CI": np.ascontiguousarray(np.asarray(L0_attn_CI, np.float32)),
        "gamma": np.asarray(L0_attn_gamma, np.float32),
        "beta": np.asarray(L0_attn_beta, np.float32),
        "Dvec": np.asarray(L0_attn_D, np.float32),
        "LR": np.asarray(L0_attn_LR, np.float32),
        "LI": np.asarray(L0_attn_LI, np.float32),
    }
    in_maps = []
    for k in range(NCORE):
        npad = L - LC - LC * k
        xs = np.concatenate([
            np.zeros((npad, H), np.float32),
            x[0:LC * k],
            x[LC * k:LC * (k + 1)],
        ], axis=0)
        mask = np.zeros((128, NPRE), np.float32)
        first_real = NPRE - (LC * k) // T
        mask[:, first_real:] = 1.0
        m = dict(com)
        m["x_shift"] = xs
        m["mask_pre"] = mask
        in_maps.append(m)

    if _NC_CACHE[0] is None:
        _NC_CACHE[0] = build_program()
    nc = _NC_CACHE[0]
    res = run_bass_kernel_spmd(nc, in_maps, core_ids=list(range(NCORE)))
    out = np.concatenate([np.asarray(res.results[k]["out"], np.float32)
                          for k in range(NCORE)], axis=0)
    return out.reshape(1, L, H)


if __name__ == "__main__":
    nc = build_program()
    print("build ok")


# version 10 (final)
# speedup vs baseline: 1.0536x; 1.0536x over previous
import math
import numpy as np
import ml_dtypes
import concourse.bass as bass
import concourse.mybir as mybir
from concourse.bass_utils import run_bass_kernel_spmd

F32 = mybir.dt.float32
BF16 = mybir.dt.bfloat16
I32 = mybir.dt.int32
AF = mybir.ActivationFunctionType
OP = mybir.AluOpType

H = 1024
L = 16384
NCORE = 8
LC = L // NCORE
T = 128
NOWN = LC // T
NPRE = (L - LC) // T
NSUP = NPRE // 8
NST = 64
E = 8
F = 4096
Q2 = 32
EPS = 1e-5
JIT2 = 0.02
TWO_PI = float(np.float32(2.0 * math.pi))
TWO_PI_ERR = float(np.float64(TWO_PI) - 2.0 * math.pi)
HALF_PI = 0.5 * math.pi
PI_C = float(np.float32(math.pi))

ENGS = ("sync", "gpsimd", "scalar", "vector", "tensor")
DEBUG_TAPS = False


class Prog:

    def __init__(self, semnames):
        self.q = {e: [] for e in ENGS}
        self.cnt = {s: 0 for s in semnames}
        self.lastwait = {e: {s: -1 for s in semnames} for e in ENGS}

    def emit(self, eng, fn, waits=(), inc=None):
        w = []
        for s, v in waits:
            assert v <= self.cnt[s], f"wait on future: {s} {v} > {self.cnt[s]}"
            if v > self.lastwait[eng][s] or eng == "gpsimd":
                w.append((s, v))
                self.lastwait[eng][s] = max(v, self.lastwait[eng][s])
        if inc is not None:
            self.cnt[inc[0]] += inc[1]
        self.q[eng].append((w, fn, inc))

    def c(self, sem):
        return self.cnt[sem]


def build_program(limit=99, sub=99, dlim=E):
    nc = bass.Bass("TRN2", target_bir_lowering=False)

    d_x = nc.dram_tensor("x_shift", [L, H], F32, kind="ExternalInput").ap()
    d_mask = nc.dram_tensor("mask_pre", [128, NPRE], F32, kind="ExternalInput").ap()
    d_gw = nc.dram_tensor("gate_wT", [H, E], F32, kind="ExternalInput").ap()
    d_w1 = nc.dram_tensor("w1T", [E, H, F], F32, kind="ExternalInput").ap()
    d_w2 = nc.dram_tensor("w2T", [E, F, H], F32, kind="ExternalInput").ap()
    d_w3 = nc.dram_tensor("w3T", [E, H, F], F32, kind="ExternalInput").ap()
    d_cr = nc.dram_tensor("CR", [H, NST], F32, kind="ExternalInput").ap()
    d_ci = nc.dram_tensor("CI", [H, NST], F32, kind="ExternalInput").ap()
    d_gamma = nc.dram_tensor("gamma", [H], F32, kind="ExternalInput").ap()
    d_beta = nc.dram_tensor("beta", [H], F32, kind="ExternalInput").ap()
    d_D = nc.dram_tensor("Dvec", [H], F32, kind="ExternalInput").ap()
    d_lr = nc.dram_tensor("LR", [NST], F32, kind="ExternalInput").ap()
    d_li = nc.dram_tensor("LI", [NST], F32, kind="ExternalInput").ap()
    d_out = nc.dram_tensor("out", [LC, H], F32, kind="ExternalOutput").ap()
    if DEBUG_TAPS:
        d_tap1 = nc.dram_tensor("tap1", [128, 16], F32, kind="ExternalOutput").ap()
        d_tap2 = nc.dram_tensor("tap2", [128, 1024], F32, kind="ExternalOutput").ap()
        d_tap3 = nc.dram_tensor("tap3", [128, 1024], F32, kind="ExternalOutput").ap()
        d_tap4 = nc.dram_tensor("tap4", [128, 128], F32, kind="ExternalOutput").ap()
        d_tap5 = nc.dram_tensor("tap5", [128, 128], F32, kind="ExternalOutput").ap()

    A = nc.alloc_sbuf_tensor
    ACC = A("ACC", [128, NOWN * H], F32)
    XNFM = A("XNFM", [128, 8 * LC], BF16)
    POOLB = A("POOLB", [128, 32768], BF16)
    POOLF = A("POOLF", [128, 4224], F32)
    GSB = A("GSB", [128, 2 * 4096], BF16)
    YD32 = A("YD32", [128, 2176], F32)
    SILT = A("SILT", [128, 2 * 512], BF16)
    K32 = A("K32", [128, 8 * Q2 + NOWN * E], F32)
    KI32 = A("KI32", [128, 1], I32)

    XN2TM = POOLB[:, 0:16384]
    W1B = POOLB[:, 0:12288]
    W3B = POOLB[:, 12288:24576]
    W2B = POOLB[:, 24576:32768]
    PRECHB = XN2TM[:, 0:1024]
    CHB = POOLB[:, 16384:17408]
    XNLOF = POOLB[:, 17408:18432]
    OWNB = POOLB[:, 18432:20480]
    A1EQ = OWNB[:, 0:1024]
    CNR = POOLB[0:NST, 20480:21504]
    CNI = POOLB[0:NST, 21504:22528]
    G1S = POOLB[0:NST, 22528:23296]
    A1PK = POOLB[:, 23296:23424]
    A2L = POOLB[:, 23424:23808]
    GATEH = POOLB[:, 23808:23872]
    GATEL = POOLB[:, 23872:23936]
    MASKB = POOLB[:, 23936:24048]
    IDB = POOLB[:, 24048:24176]
    P1RB = POOLB[0:NST, 24176:24208]
    P1IB = POOLB[0:NST, 24208:24240]
    G1RB = POOLB[0:NST, 24240:24368]
    G1INB = POOLB[0:NST, 24368:24496]
    XF2 = POOLB[:, 24496:25520]
    DXF2 = POOLB[:, 25520:26544]
    OB2 = POOLB[:, 26544:27568]
    STG1 = CNR[:, :]
    STG2 = CNI[:, :]
    CSR = XNLOF[0:NST, :]
    CSI = CHB[0:NST, :]
    CNINB = CSI[:, :]

    VTMP = POOLF[:, 0:1024]
    XSTG = POOLF[:, 1024:3072]
    SQT = POOLF[:, 3072:4096]
    EQC = POOLF[0:NST, 4096:4100]
    GCOL = POOLF[:, 4100:4108]
    BCOL = POOLF[:, 4108:4116]
    LAM = POOLF[0:NST, 4116:4132]
    PTAB = XSTG[:, 0:512]
    P1R = PTAB[0:NST, 0:128]
    P1I = PTAB[0:NST, 128:256]
    A1R = PTAB[0:NST, 256:384]
    A1I = PTAB[0:NST, 384:512]
    IDF = YD32[:, 2048:2176]
    STAT = POOLF[:, 4132:4148]
    DROW = POOLF[:, 4148:4156]
    SRS = POOLF[:, 4156:4172]
    EQ8 = POOLF[:, 4172:4214]
    GATEF = VTMP[:, 0:64]
    GBt = XSTG[:, 0:1024]
    BBt = XSTG[:, 1024:2048]
    RSC = XSTG[:, 0:1152]
    RSM = SQT[:, 128:256]
    WTOKSB = SQT[:, 256:384]
    SCORES = SQT[:, 0:128]
    SR_ = XSTG[0:NST, 0:1024]
    SI_ = XSTG[0:NST, 1024:2048]
    WTOKD = K32[:, 8 * Q2:8 * Q2 + NOWN * E]
    CR_STG = YD32[:, 0:512]
    CI_STG = YD32[:, 512:1024]
    SINR = YD32[:, 0:1024]
    CX2R = GSB[0:NST, 0:3072]
    CX2I = GSB[0:NST, 3072:6144]
    X2FM = SILT[:, :]

    PS = [nc.alloc_psum_tensor(f"PSF{i}", [128, 512], F32) for i in range(7)]
    PBT = nc.alloc_psum_tensor("PBT", [128, 1024], BF16)

    semnames = ["sX", "sWD", "sSC", "sVE", "sTE", "sST", "sGP",
                "sXa", "sXb", "sW0", "sW1", "sW2", "sWM", "sV2a", "sV2b"]
    sems = {n: nc.alloc_semaphore(n) for n in semnames}
    P = Prog(semnames)
    em = P.emit
    V, SC, TE, SY, GP = "vector", "scalar", "tensor", "sync", "gpsimd"

    def dma(eng, out_ap, in_ap, waits=(), sem="sX", slow=False):
        em(eng, lambda e, o=out_ap, i=in_ap, s=slow: e.dma_start(
            out=o, in_=i, allow_slow_non_contiguous=s),
           waits=waits, inc=(sem, 16))

    def finish():
        em(SY, lambda s: s.dma_start(out=d_out.rearrange("(j p) h -> p j h", p=128),
                                     in_=ACC[:, :].rearrange("p (j h) -> p j h", h=H)),
           waits=[("sVE", P.c("sVE")), ("sTE", P.c("sTE")), ("sSC", P.c("sSC")),
                  ("sWD", P.c("sWD")), ("sX", P.c("sX")), ("sGP", P.c("sGP")),
                  ("sXa", P.c("sXa")), ("sXb", P.c("sXb")), ("sW0", P.c("sW0")),
                  ("sW1", P.c("sW1")), ("sW2", P.c("sW2")), ("sWM", P.c("sWM")),
                  ("sV2a", P.c("sV2a")), ("sV2b", P.c("sV2b"))],
           inc=("sST", 16))
        em(SY, lambda s: s.wait_ge(sems["sST"], P.c("sST")))
        with nc.Block() as block:
            def mk(engname):
                def run(eng):
                    for w, fn, inc in P.q[engname]:
                        for s, v in w:
                            eng.wait_ge(sems[s], v)
                        ins = fn(eng)
                        if inc is not None:
                            assert ins is not None, f"{engname} op needs inst for inc"
                            ins.then_inc(sems[inc[0]], inc[1])
                return run
            block.sync(mk("sync"))
            block.gpsimd(mk("gpsimd"))
            block.scalar(mk("scalar"))
            block.vector(mk("vector"))
            block.tensor(mk("tensor"))
        return nc
    finish_hook = [finish]
    dma(SY, GCOL[:], d_gamma.rearrange("(hc p) -> p hc", p=128), slow=True)
    dma(SY, BCOL[:], d_beta.rearrange("(hc p) -> p hc", p=128), slow=True)
    dma(SY, DROW[:], d_D.rearrange("(hc p) -> p hc", p=128), slow=True)
    dma(SY, LAM[:, 0:1], d_lr.rearrange("(p one) -> p one", one=1), slow=True)
    dma(SY, LAM[:, 1:2], d_li.rearrange("(p one) -> p one", one=1), slow=True)
    dma(SY, GATEF[:].rearrange("p (hc e) -> p hc e", e=E),
        d_gw.rearrange("(hc p) e -> p hc e", p=128), slow=True)
    dma(SY, CR_STG.rearrange("p (hc n) -> p hc n", n=NST),
        d_cr.rearrange("(hc p) n -> p hc n", p=128))
    dma(SY, CI_STG.rearrange("p (hc n) -> p hc n", n=NST),
        d_ci.rearrange("(hc p) n -> p hc n", p=128))
    dma(SY, GBt, d_gamma.unsqueeze(0).partition_broadcast(128))
    dma(SY, BBt, d_beta.unsqueeze(0).partition_broadcast(128))
    ld_small = P.c("sX")
    dma(GP, MASKB[:], d_mask, sem="sWM")
    ld_maskb = P.c("sWM")

    em(V, lambda v: (v.memset(IDB[:], 1.0), v.memset(IDF[:], 1.0),
                     v.drain())[-1], inc=("sVE", 1))
    ve_idm = P.c("sVE")
    em(GP, lambda g: g.affine_select(IDB[:], IDB[:], [[-1, 128]], OP.is_equal,
                                     0.0, base=0, channel_multiplier=1),
       waits=[("sVE", ve_idm)], inc=("sGP", 1))
    em(GP, lambda g: g.affine_select(IDF[:], IDF[:], [[-1, 128]], OP.is_equal,
                                     0.0, base=0, channel_multiplier=1),
       waits=[("sVE", ve_idm)], inc=("sGP", 1))
    gp_ident_gp = P.c("sGP")

    if sub <= 1:
        return finish()
    wsm = [("sX", ld_small)]
    em(V, lambda v: (v.tensor_copy(GATEH[:], GATEF[:]), v.drain())[0], waits=wsm)
    em(V, lambda v: (v.tensor_sub(GATEL[:], GATEF[:], GATEH[:]), v.drain())[0])
    ld_dgb = P.c("sX")

    em(SC, lambda s: s.activation(LAM[:, 2:3], LAM[:, 0:1], AF.Exp), waits=wsm)
    em(SC, lambda s: s.activation(LAM[:, 3:4], LAM[:, 1:2], AF.Exp), inc=("sSC", 1))
    em(V, lambda v: (v.tensor_scalar(LAM[:, 2:3], LAM[:, 2:3], -1.0, None, OP.mult), v.drain())[0],
       waits=[("sSC", P.c("sSC"))])
    em(V, lambda v: (v.tensor_copy(LAM[:, 8:9], LAM[:, 2:3]), v.drain())[0], inc=("sVE", 1))
    em(SC, lambda s: s.activation(LAM[:, 9:10], LAM[:, 8:9], AF.Exp),
       waits=[("sVE", P.c("sVE"))], inc=("sSC", 1))
    em(V, lambda v: (v.tensor_scalar(LAM[:, 10:11], LAM[:, 3:4], 1.0 / TWO_PI, 0.5,
                                     OP.mult, OP.add), v.drain())[0],
       waits=[("sSC", P.c("sSC"))])
    em(V, lambda v: (v.tensor_copy(KI32[0:NST, 0:1], LAM[:, 10:11]), v.drain())[0])
    em(V, lambda v: (v.tensor_copy(LAM[:, 10:11], KI32[0:NST, 0:1]), v.drain())[0])
    em(V, lambda v: (v.scalar_tensor_tensor(LAM[:, 12:13], LAM[:, 10:11], -TWO_PI,
                                           LAM[:, 3:4], OP.mult, OP.add), v.drain())[0])
    em(V, lambda v: (v.scalar_tensor_tensor(LAM[:, 12:13], LAM[:, 10:11], TWO_PI_ERR,
                                           LAM[:, 12:13], OP.mult, OP.add), v.drain())[0])
    em(V, lambda v: (v.tensor_scalar(LAM[:, 11:12], LAM[:, 12:13], PI_C, None, OP.is_gt), v.drain())[0])
    em(V, lambda v: (v.scalar_tensor_tensor(LAM[:, 12:13], LAM[:, 11:12], -TWO_PI,
                                           LAM[:, 12:13], OP.mult, OP.add), v.drain())[0])
    em(V, lambda v: (v.scalar_tensor_tensor(LAM[:, 12:13], LAM[:, 11:12], TWO_PI_ERR,
                                           LAM[:, 12:13], OP.mult, OP.add), v.drain())[0])
    em(V, lambda v: (v.tensor_scalar(LAM[:, 11:12], LAM[:, 12:13], -1.0, PI_C,
                                     OP.mult, OP.is_gt), v.drain())[0])
    em(V, lambda v: (v.scalar_tensor_tensor(LAM[:, 12:13], LAM[:, 11:12], TWO_PI,
                                           LAM[:, 12:13], OP.mult, OP.add), v.drain())[0])
    em(V, lambda v: (v.scalar_tensor_tensor(LAM[:, 12:13], LAM[:, 11:12], -TWO_PI_ERR,
                                           LAM[:, 12:13], OP.mult, OP.add), v.drain())[0])
    em(V, lambda v: (v.tensor_scalar(LAM[:, 13:14], LAM[:, 12:13], HALF_PI, None, OP.add), v.drain())[0])
    em(V, lambda v: (v.tensor_scalar(LAM[:, 11:12], LAM[:, 13:14], PI_C, None, OP.is_gt), v.drain())[0])
    em(V, lambda v: (v.scalar_tensor_tensor(LAM[:, 13:14], LAM[:, 11:12], -TWO_PI,
                                           LAM[:, 13:14], OP.mult, OP.add), v.drain())[0])
    em(V, lambda v: (v.scalar_tensor_tensor(LAM[:, 13:14], LAM[:, 11:12], TWO_PI_ERR,
                                           LAM[:, 13:14], OP.mult, OP.add), v.drain())[0], inc=("sVE", 1))
    em(SC, lambda s: s.activation(LAM[:, 10:11], LAM[:, 12:13], AF.Sin),
       waits=[("sVE", P.c("sVE"))])
    em(SC, lambda s: s.activation(LAM[:, 11:12], LAM[:, 13:14], AF.Sin), inc=("sSC", 1))
    em(V, lambda v: (v.tensor_mul(LAM[:, 4:5], LAM[:, 9:10], LAM[:, 11:12]), v.drain())[0],
       waits=[("sSC", P.c("sSC"))])
    em(V, lambda v: (v.tensor_mul(LAM[:, 5:6], LAM[:, 9:10], LAM[:, 10:11]), v.drain())[0])

    def q_ops(v):
        v.tensor_scalar(LAM[:, 8:9], LAM[:, 4:5], -1.0, None, OP.add)
        v.drain()
        v.tensor_mul(LAM[:, 9:10], LAM[:, 2:3], LAM[:, 2:3])
        v.drain()
        v.tensor_mul(LAM[:, 10:11], LAM[:, 3:4], LAM[:, 3:4])
        v.drain()
        v.tensor_add(LAM[:, 9:10], LAM[:, 9:10], LAM[:, 10:11])
        v.drain()
        v.reciprocal(LAM[:, 9:10], LAM[:, 9:10])
        v.drain()
        v.tensor_mul(LAM[:, 10:11], LAM[:, 8:9], LAM[:, 2:3])
        v.drain()
        v.tensor_mul(LAM[:, 11:12], LAM[:, 5:6], LAM[:, 3:4])
        v.drain()
        v.tensor_add(LAM[:, 10:11], LAM[:, 10:11], LAM[:, 11:12])
        v.drain()
        v.tensor_mul(LAM[:, 6:7], LAM[:, 10:11], LAM[:, 9:10])
        v.drain()
        v.tensor_mul(LAM[:, 10:11], LAM[:, 5:6], LAM[:, 2:3])
        v.drain()
        v.tensor_mul(LAM[:, 11:12], LAM[:, 8:9], LAM[:, 3:4])
        v.drain()
        v.tensor_sub(LAM[:, 10:11], LAM[:, 10:11], LAM[:, 11:12])
        v.drain()
        v.tensor_mul(LAM[:, 7:8], LAM[:, 10:11], LAM[:, 9:10])
        return v.drain()
    em(V, q_ops)

    def cxsq_running(v):
        v.tensor_mul(SQT[0:NST, 0:1], LAM[:, 14:15], LAM[:, 14:15])
        v.tensor_mul(SQT[0:NST, 1:2], LAM[:, 15:16], LAM[:, 15:16])
        v.tensor_mul(SQT[0:NST, 2:3], LAM[:, 14:15], LAM[:, 15:16])
        v.drain()
        v.tensor_sub(LAM[:, 14:15], SQT[0:NST, 0:1], SQT[0:NST, 1:2])
        v.tensor_scalar(LAM[:, 15:16], SQT[0:NST, 2:3], 2.0, None, OP.mult)
        return v.drain()

    def p1_build(v):
        v.memset(P1R[:, 0:1], 1.0)
        v.drain()
        v.memset(P1I[:, 0:1], 0.0)
        v.drain()
        v.tensor_copy(LAM[:, 14:15], LAM[:, 4:5])
        v.drain()
        v.tensor_copy(LAM[:, 15:16], LAM[:, 5:6])
        v.drain()
        blk = 1
        while blk < T:
            m = min(blk, T - blk)
            v.tensor_scalar(SQT[0:NST, 0:m], P1R[:, 0:m], LAM[:, 14:15], None, OP.mult)
            v.drain()
            v.tensor_scalar(SQT[0:NST, 256:256 + m], P1I[:, 0:m], LAM[:, 15:16], None, OP.mult)
            v.drain()
            v.tensor_sub(P1R[:, blk:blk + m], SQT[0:NST, 0:m], SQT[0:NST, 256:256 + m])
            v.drain()
            v.tensor_scalar(SQT[0:NST, 0:m], P1R[:, 0:m], LAM[:, 15:16], None, OP.mult)
            v.drain()
            v.tensor_scalar(SQT[0:NST, 256:256 + m], P1I[:, 0:m], LAM[:, 14:15], None, OP.mult)
            v.drain()
            v.tensor_add(P1I[:, blk:blk + m], SQT[0:NST, 0:m], SQT[0:NST, 256:256 + m])
            v.drain()
            cxsq_running(v)
            blk *= 2
        v.tensor_copy(EQC[:, 0:1], LAM[:, 14:15])
        v.drain()
        v.tensor_copy(EQC[:, 1:2], LAM[:, 15:16])
        v.drain()
        v.tensor_scalar(EQC[:, 2:3], LAM[:, 15:16], -1.0, None, OP.mult)
        return v.drain()
    em(V, p1_build)

    def a1_build(v):
        for c in range(T):
            v.tensor_copy(A1R[:, c:c + 1], P1R[:, T - 1 - c:T - c])
            v.tensor_copy(A1I[:, c:c + 1], P1I[:, T - 1 - c:T - c])
        return v.drain()
    em(V, a1_build)

    def g1_build(v):
        v.tensor_scalar(SQT[0:NST, 0:T], P1R[:, :], LAM[:, 4:5], None, OP.mult)
        v.tensor_scalar(SQT[0:NST, 256:256 + T], P1I[:, :], LAM[:, 5:6], None, OP.mult)
        v.drain()
        v.tensor_sub(SQT[0:NST, 512:512 + T], SQT[0:NST, 0:T], SQT[0:NST, 256:256 + T])
        v.drain()
        v.tensor_copy(G1RB[:, :], SQT[0:NST, 512:512 + T])
        v.drain()
        v.tensor_scalar(SQT[0:NST, 0:T], P1R[:, :], LAM[:, 5:6], None, OP.mult)
        v.tensor_scalar(SQT[0:NST, 256:256 + T], P1I[:, :], LAM[:, 4:5], None, OP.mult)
        v.drain()
        v.tensor_add(SQT[0:NST, 512:512 + T], SQT[0:NST, 0:T], SQT[0:NST, 256:256 + T])
        v.drain()
        v.tensor_scalar(SQT[0:NST, 512:512 + T], SQT[0:NST, 512:512 + T], -1.0, None, OP.mult)
        v.drain()
        v.tensor_copy(G1INB[:, :], SQT[0:NST, 512:512 + T])
        return v.drain()
    em(V, g1_build)

    def g1s_build(v):
        v.memset(G1S[:, :], 0.0)
        ins = None
        v.drain()
        for m in (1, 2, 3):
            sh = 32 * m
            v.tensor_copy(G1S[:, (m - 1) * T + sh:m * T], G1RB[:, 0:T - sh])
            v.tensor_copy(G1S[:, (2 + m) * T + sh:(3 + m) * T], G1INB[:, 0:T - sh])
        return v.drain()
    em(V, g1s_build)

    def eq8_build(v):
        v.tensor_copy(LAM[:, 14:15], EQC[:, 0:1])
        v.drain()
        v.tensor_copy(LAM[:, 15:16], EQC[:, 1:2])
        v.drain()
        for _ in range(3):
            cxsq_running(v)
        v.memset(EQ8[0:NST, 13:14], 1.0)
        v.drain()
        v.memset(EQ8[0:NST, NSUP + 13:NSUP + 14], 0.0)
        v.drain()
        for p in range(1, NSUP):
            cdst, csrc = 13 - p, 14 - p
            v.tensor_scalar(SQT[0:NST, 0:1], EQ8[0:NST, csrc:csrc + 1], LAM[:, 14:15], None, OP.mult)
            v.drain()
            v.tensor_scalar(SQT[0:NST, 1:2], EQ8[0:NST, NSUP + csrc:NSUP + csrc + 1], LAM[:, 15:16], None, OP.mult)
            v.drain()
            v.tensor_sub(EQ8[0:NST, cdst:cdst + 1], SQT[0:NST, 0:1], SQT[0:NST, 1:2])
            v.drain()
            v.tensor_scalar(SQT[0:NST, 0:1], EQ8[0:NST, csrc:csrc + 1], LAM[:, 15:16], None, OP.mult)
            v.drain()
            v.tensor_scalar(SQT[0:NST, 1:2], EQ8[0:NST, NSUP + csrc:NSUP + csrc + 1], LAM[:, 14:15], None, OP.mult)
            v.drain()
            v.tensor_add(EQ8[0:NST, NSUP + cdst:NSUP + cdst + 1], SQT[0:NST, 0:1], SQT[0:NST, 1:2])
            v.drain()
        v.tensor_scalar(EQ8[0:NST, 2 * NSUP:3 * NSUP], EQ8[0:NST, NSUP:2 * NSUP], -1.0, None, OP.mult)
        v.drain()
        v.tensor_copy(EQ8[64:128, :], EQ8[0:NST, :])
        return v.drain()
    em(V, eq8_build)

    def a1eq_stage(v):
        v.memset(LAM[:, 14:15], 1.0)
        v.drain()
        v.memset(LAM[:, 15:16], 0.0)
        v.drain()
        for p in range(8):
            i = 7 - p
            v.tensor_scalar(SQT[0:NST, 0:T], A1R[:, :], LAM[:, 14:15], None, OP.mult)
            v.tensor_scalar(SQT[0:NST, 256:256 + T], A1I[:, :], LAM[:, 15:16], None, OP.mult)
            v.drain()
            v.tensor_sub(SQT[0:NST, 512:512 + T], SQT[0:NST, 0:T], SQT[0:NST, 256:256 + T])
            v.drain()
            v.tensor_copy(STG1[:, i * T:(i + 1) * T], SQT[0:NST, 512:512 + T])
            v.drain()
            v.tensor_scalar(SQT[0:NST, 0:T], A1R[:, :], LAM[:, 15:16], None, OP.mult)
            v.tensor_scalar(SQT[0:NST, 256:256 + T], A1I[:, :], LAM[:, 14:15], None, OP.mult)
            v.drain()
            v.tensor_add(SQT[0:NST, 512:512 + T], SQT[0:NST, 0:T], SQT[0:NST, 256:256 + T])
            v.drain()
            v.tensor_copy(STG2[:, i * T:(i + 1) * T], SQT[0:NST, 512:512 + T])
            v.drain()
            if p < 7:
                v.tensor_scalar(SQT[0:NST, 0:1], LAM[:, 14:15], EQC[:, 0:1], None, OP.mult)
                v.drain()
                v.tensor_scalar(SQT[0:NST, 1:2], LAM[:, 15:16], EQC[:, 1:2], None, OP.mult)
                v.drain()
                v.tensor_sub(SQT[0:NST, 2:3], SQT[0:NST, 0:1], SQT[0:NST, 1:2])
                v.drain()
                v.tensor_scalar(SQT[0:NST, 0:1], LAM[:, 14:15], EQC[:, 1:2], None, OP.mult)
                v.drain()
                v.tensor_scalar(SQT[0:NST, 1:2], LAM[:, 15:16], EQC[:, 0:1], None, OP.mult)
                v.drain()
                v.tensor_add(LAM[:, 15:16], SQT[0:NST, 0:1], SQT[0:NST, 1:2])
                v.drain()
                v.tensor_copy(LAM[:, 14:15], SQT[0:NST, 2:3])
                v.drain()
        v.tensor_copy(CSR[:, 0:T], A1R[:, :])
        v.tensor_copy(CSI[:, 0:T], A1I[:, :])
        v.tensor_copy(CSR[:, T:T + Q2], A1R[:, T - Q2:T])
        v.tensor_copy(CSI[:, T:T + Q2], A1I[:, T - Q2:T])
        v.tensor_copy(P1RB[:, :], P1R[:, 0:Q2])
        v.tensor_copy(P1IB[:, :], P1I[:, 0:Q2])
        return v.drain()
    em(V, a1eq_stage, inc=("sVE", 1))
    ve_stage = P.c("sVE")

    if sub <= 3:
        return finish()
    def tr_pack(dst_ap, srcR_ap, srcI_ap, first_waits=()):
        em(TE, lambda t, a=srcR_ap: t.transpose(PBT[:, 0:64], a, IDB[0:64, 0:64]),
           waits=list(first_waits))
        em(TE, lambda t, a=srcI_ap: t.transpose(PBT[:, 64:128], a, IDB[0:64, 0:64]),
           inc=("sTE", 1))
        mt = P.c("sTE")
        em(V, lambda v, d=dst_ap: v.tensor_copy(d, PBT[:, 0:128]),
           waits=[("sTE", mt)], inc=("sVE", 1))
        return P.c("sVE")

    w0 = [("sVE", ve_stage), ("sGP", gp_ident_gp)]
    lastev = None
    for i in range(8):
        w = w0 if lastev is None else [("sVE", lastev)]
        lastev = tr_pack(A1EQ[:, i * 128:(i + 1) * 128],
                         STG1[:, i * T:(i + 1) * T], STG2[:, i * T:(i + 1) * T], w)
    lastev = tr_pack(A1PK[:, :], CSR[:, 0:T], CSI[:, 0:T], [("sVE", lastev)])
    em(TE, lambda t: t.transpose(PBT[0:32, 0:64], CSR[:, T:T + Q2], IDB[0:64, 0:64]),
       waits=[("sVE", lastev)])
    em(TE, lambda t: t.transpose(PBT[0:32, 64:128], CSI[:, T:T + Q2], IDB[0:64, 0:64]),
       inc=("sTE", 1))
    mt = P.c("sTE")
    def a2l_fill(v):
        v.memset(A2L[:, :], 0.0)
        v.drain()
        for i in range(3):
            v.tensor_copy(A2L[32 * i:32 * i + 32, i * 128:i * 128 + 128],
                          PBT[0:32, 0:128])
        return v.drain()
    em(V, a2l_fill, waits=[("sTE", mt)], inc=("sVE", 1))
    ve_a2l = P.c("sVE")

    if sub <= 4:
        return finish()
    for hc in range(8):
        em(TE, lambda t, hc=hc: t.matmul(PS[0][0:64, 0:128],
                                         CR_STG[:, hc * 64:(hc + 1) * 64],
                                         IDF[:], start=True, stop=True),
           waits=([("sVE", ve_a2l)] if hc == 0 else [("sVE", P.c("sVE"))])
                 + [("sX", ld_small)],
           inc=("sTE", 1))
        mt = P.c("sTE")
        em(V, lambda v, hc=hc: v.tensor_copy(SR_[:, hc * 128:(hc + 1) * 128],
                                             PS[0][0:64, 0:128]),
           waits=[("sTE", mt)], inc=("sVE", 1))
    for hc in range(8):
        em(TE, lambda t, hc=hc: t.matmul(PS[0][0:64, 0:128],
                                         CI_STG[:, hc * 64:(hc + 1) * 64],
                                         IDF[:], start=True, stop=True),
           waits=[("sVE", P.c("sVE"))], inc=("sTE", 1))
        mt = P.c("sTE")
        em(V, lambda v, hc=hc: v.tensor_copy(SI_[:, hc * 128:(hc + 1) * 128],
                                             PS[0][0:64, 0:128]),
           waits=[("sTE", mt)], inc=("sVE", 1))

    def cn_build(v):
        v.drain()
        v.tensor_scalar(VTMP[0:NST, :], SR_[:, :], LAM[:, 6:7], None, OP.mult)
        v.tensor_scalar(SQT[0:NST, :], SI_[:, :], LAM[:, 7:8], None, OP.mult)
        v.drain()
        v.tensor_sub(VTMP[0:NST, :], VTMP[0:NST, :], SQT[0:NST, :])
        v.drain()
        v.tensor_copy(CNR[:, :], VTMP[0:NST, :])
        v.drain()
        v.tensor_scalar(VTMP[0:NST, :], SR_[:, :], LAM[:, 7:8], None, OP.mult)
        v.tensor_scalar(SQT[0:NST, :], SI_[:, :], LAM[:, 6:7], None, OP.mult)
        v.drain()
        v.tensor_add(VTMP[0:NST, :], VTMP[0:NST, :], SQT[0:NST, :])
        v.drain()
        v.tensor_copy(CNI[:, :], VTMP[0:NST, :])
        v.drain()
        v.tensor_scalar(VTMP[0:NST, :], VTMP[0:NST, :], -1.0, None, OP.mult)
        v.drain()
        v.tensor_copy(CNINB[:, :], VTMP[0:NST, :])
        v.memset(SINR[:, :], 0.0)
        v.memset(SRS[:, :], 0.0)
        return v.drain()
    em(V, cn_build, inc=("sVE", 1))
    ve_cn = P.c("sVE")

    if sub <= 5:
        return finish()
    for hc in range(8):
        em(TE, lambda t, hc=hc: t.matmul(PS[0][:, hc * 32:(hc + 1) * 32],
                                         CNR[:, hc * 128:(hc + 1) * 128],
                                         P1RB[:, :], start=True, stop=False),
           waits=[("sVE", ve_cn)])
        em(TE, lambda t, hc=hc: t.matmul(PS[0][:, hc * 32:(hc + 1) * 32],
                                         CNINB[:, hc * 128:(hc + 1) * 128],
                                         P1IB[:, :], start=False, stop=True),
           inc=(("sTE", 1) if hc == 7 else None))
    k32_mt = P.c("sTE")
    em(V, lambda v: v.tensor_copy(K32[:, 0:8 * Q2], PS[0][:, 0:256]),
       waits=[("sTE", k32_mt)], inc=("sVE", 1))
    ve_k32 = P.c("sVE")

    if limit <= 0:
        return finish()
    slot_v = [None, None]
    slot_s = [None, None]
    xm = {}
    chb_mm = {}
    hi_mm = {}
    lo_mm = {}
    ob2_mm = {}
    x2_mm = {}
    dx_mm = {}
    rec_j_ms = {}
    sup_read = None
    score_group = 0
    own_lo_ready = {}
    own_hi_ready = {}
    xn2_ready = {}
    chb_ready = {}
    stat_free_s = None
    prev_affine_s = None
    prev_vec_fin = None
    hi_evac = {}
    lo_evac = {}
    score_ms = {}
    sq_free_v = None
    pbt_free = lastev
    pbt_free = P.c("sVE")

    for c in range(128):
        own = c >= NPRE
        j = c - NPRE
        slot = c % 2
        sup = c // 8 if not own else None
        w = []
        if slot_v[slot] is not None:
            w.append(("sVE", slot_v[slot]))
        if slot_s[slot] is not None:
            w.append(("sSC", slot_s[slot]))
        if c < 2:
            w.append(("sX", ld_dgb))
            w.append(("sVE", ve_cn))
        xs = XSTG[:, slot * 1024:(slot + 1) * 1024]
        xsem = "sXa" if slot == 0 else "sXb"
        dma(SY, xs, d_x[c * T:(c + 1) * T, :], waits=w, sem=xsem)
        xm[c] = (xsem, P.c(xsem))

        wv = [xm[c]]
        if stat_free_s is not None:
            wv.append(("sSC", stat_free_s))
        em(V, lambda v, xs=xs: v.reduce_sum(STAT[:, 0:1], xs, axis=mybir.AxisListType.X),
           waits=wv, inc=("sVE", 1))
        m_red = P.c("sVE")
        wsq = [xm[c], ("sVE", m_red)]
        if sq_free_v is not None and own:
            pass
        if sq_free_v is not None:
            wsq.append(("sVE", sq_free_v))
        em(SC, lambda s, xs=xs: s.activation(SQT, xs, AF.Square, accum_out=STAT[:, 1:2]),
           waits=wsq, inc=("sSC", 1))
        m_sq = P.c("sSC")

        def stats1(v):
            v.tensor_scalar(STAT[:, 6:7], STAT[:, 0:1], 1.0 / H, None, OP.mult)
            v.drain()
            v.tensor_scalar(STAT[:, 7:8], STAT[:, 1:2], 1.0 / H, None, OP.mult)
            v.drain()
            v.tensor_mul(STAT[:, 13:14], STAT[:, 6:7], STAT[:, 6:7])
            v.drain()
            v.tensor_sub(STAT[:, 7:8], STAT[:, 7:8], STAT[:, 13:14])
            v.drain()
            v.tensor_scalar(STAT[:, 2:3], STAT[:, 7:8], EPS, None, OP.add)
            v.drain()
            v.reciprocal(STAT[:, 14:15], STAT[:, 2:3])
            v.drain()
            v.tensor_mul(STAT[:, 13:14], STAT[:, 7:8], STAT[:, 14:15])
            v.drain()
            v.tensor_scalar(STAT[:, 3:4], STAT[:, 13:14], EPS, None, OP.add)
            return v.drain()
        em(V, stats1, waits=[("sSC", m_sq)], inc=("sVE", 1))
        m_st1 = P.c("sVE")
        em(SC, lambda s: s.activation(STAT[:, 4:6], STAT[:, 2:4], AF.Sqrt),
           waits=[("sVE", m_st1)], inc=("sSC", 1))
        m_sqrt = P.c("sSC")

        def stats2(v, c=c, own=own):
            v.reciprocal(STAT[:, 8:10], STAT[:, 4:6])
            v.drain()
            v.tensor_mul(STAT[:, 13:15], STAT[:, 2:4], STAT[:, 8:10])
            v.drain()
            v.tensor_add(STAT[:, 4:6], STAT[:, 4:6], STAT[:, 13:15])
            v.drain()
            v.tensor_scalar(STAT[:, 4:6], STAT[:, 4:6], 0.5, None, OP.mult)
            v.drain()
            v.reciprocal(STAT[:, 8:10], STAT[:, 4:6])
            v.drain()
            v.tensor_mul(STAT[:, 10:11], STAT[:, 8:9], STAT[:, 9:10])
            v.drain()
            if not own:
                v.tensor_mul(STAT[:, 10:11], STAT[:, 10:11], MASKB[:, c:c + 1])
                v.drain()
            v.tensor_mul(STAT[:, 11:12], STAT[:, 6:7], STAT[:, 10:11])
            v.drain()
            v.tensor_scalar(STAT[:, 11:12], STAT[:, 11:12], -1.0, None, OP.mult)
            v.drain()
            if own:
                v.tensor_mul(STAT[:, 12:13], STAT[:, 6:7], STAT[:, 8:9])
                v.drain()
                v.tensor_scalar(STAT[:, 12:13], STAT[:, 12:13], -1.0, None, OP.mult)
                return v.drain()
            return v.drain()
        em(V, stats2, waits=[("sSC", m_sqrt), ("sWM", ld_maskb)], inc=("sVE", 1))
        m_st2 = P.c("sVE")
        if DEBUG_TAPS and c == NPRE:
            em(SY, lambda s: s.dma_start(out=d_tap1, in_=STAT[:, :]),
               waits=[("sVE", m_st2)], inc=("sST", 16))

        if not own:
            cb = PRECHB
            wch = [("sVE", m_st2)]
            if c == 0:
                wch.append(("sTE", k32_mt))
            if c >= 1:
                wch.append(("sTE", chb_mm[c - 1]))
            em(SC, lambda s, xs=xs, cb=cb: s.activation(
                cb, xs, AF.Identity, bias=STAT[:, 11:12], scale=STAT[:, 10:11]),
               waits=wch, inc=("sSC", 1))
            chb_ready[c] = P.c("sSC")
            stat_free_s = P.c("sSC")
            wmm = [("sSC", chb_ready[c]), ("sVE", ve_k32), ("sGP", gp_ident_gp)]
            if c % 8 == 0 and sup_read is not None:
                wmm.append(("sVE", sup_read))
            i = c % 8
            first = (i == 0)
            last = (i == 7)
            em(TE, lambda t, i=i, cb=cb, first=first, last=last: t.matmul(
                PS[0][:, :], A1EQ[:, i * 128:(i + 1) * 128], cb[:, 0:512],
                start=first, stop=last, skip_group_check=True), waits=wmm)
            em(TE, lambda t, i=i, cb=cb, first=first, last=last: t.matmul(
                PS[1][:, :], A1EQ[:, i * 128:(i + 1) * 128], cb[:, 512:1024],
                start=first, stop=last, skip_group_check=True))
            em(TE, lambda t, i=i, c=c, first=first, last=last: t.matmul(
                PS[2][:, 0:1], A1EQ[:, i * 128:(i + 1) * 128], MASKB[:, c:c + 1],
                start=first, stop=last, skip_group_check=True),
               inc=("sTE", 1))
            chb_mm[c] = P.c("sTE")
            if last:
                mt = P.c("sTE")
                def sup_acc(v, s=sup):
                    for hh in range(2):
                        tgt_r = SINR[0:64, hh * 512:(hh + 1) * 512]
                        tgt_i = SINR[64:128, hh * 512:(hh + 1) * 512]
                        v.scalar_tensor_tensor(tgt_r, PS[hh][0:64, :],
                                               EQ8[0:NST, s:s + 1],
                                               tgt_r, OP.mult, OP.add)
                        v.scalar_tensor_tensor(tgt_i, PS[hh][0:64, :],
                                               EQ8[64:128, NSUP + s:NSUP + s + 1],
                                               tgt_i, OP.mult, OP.add)
                        v.drain()
                        v.scalar_tensor_tensor(tgt_r, PS[hh][64:128, :],
                                               EQ8[0:NST, 2 * NSUP + s:2 * NSUP + s + 1],
                                               tgt_r, OP.mult, OP.add)
                        v.scalar_tensor_tensor(tgt_i, PS[hh][64:128, :],
                                               EQ8[64:128, s:s + 1],
                                               tgt_i, OP.mult, OP.add)
                        v.drain()
                    v.scalar_tensor_tensor(SRS[0:64, 0:1], PS[2][0:64, 0:1],
                                           EQ8[0:NST, s:s + 1],
                                           SRS[0:64, 0:1], OP.mult, OP.add)
                    v.drain()
                    v.scalar_tensor_tensor(SRS[0:64, 0:1], PS[2][64:128, 0:1],
                                           EQ8[0:NST, 2 * NSUP + s:2 * NSUP + s + 1],
                                           SRS[0:64, 0:1], OP.mult, OP.add)
                    v.drain()
                    v.scalar_tensor_tensor(SRS[64:128, 0:1], PS[2][0:64, 0:1],
                                           EQ8[64:128, NSUP + s:NSUP + s + 1],
                                           SRS[64:128, 0:1], OP.mult, OP.add)
                    v.drain()
                    v.scalar_tensor_tensor(SRS[64:128, 0:1], PS[2][64:128, 0:1],
                                           EQ8[64:128, s:s + 1], SRS[64:128, 0:1],
                                           OP.mult, OP.add)
                    return v.drain()
                em(V, sup_acc, waits=[("sTE", mt), ("sVE", ve_cn)], inc=("sVE", 1))
                sup_read = P.c("sVE")
            slot_v[slot] = m_st2
            slot_s[slot] = P.c("sSC")
        else:
            ob = OWNB[:, (j % 2) * H:(j % 2) * H + H]
            cb = CHB[:, :]
            wsc = [("sVE", m_st2)]
            if j == 0:
                wsc.append(("sTE", chb_mm[NPRE - 1]))
            if j >= 2:
                wsc.append(("sTE", hi_mm[j - 2]))
            em(SC, lambda s, xs=xs: (s.activation(SQT, xs, AF.Identity,
                                                  bias=STAT[:, 12:13], scale=STAT[:, 8:9]),
                                     s.drain())[0], waits=wsc)
            em(SC, lambda s, ob=ob: s.activation(ob, SQT, AF.Identity),
               inc=("sSC", 1))
            own_hi_ready[j] = P.c("sSC")
            wob2 = []
            if j >= 1:
                wob2.append(("sTE", ob2_mm[j - 1]))
            em(SC, lambda s: s.activation(OB2[:, :], SQT, AF.Identity,
                                          scale=STAT[:, 9:10]),
               waits=wob2, inc=("sSC", 1))
            ob2_ready = P.c("sSC")
            stat_free_s = ob2_ready
            wlo = [("sSC", own_hi_ready[j])]
            if j == 0:
                wlo.append(("sTE", chb_mm[NPRE - 1]))
            if j >= 1:
                wlo.append(("sTE", lo_mm[j - 1]))
            em(V, lambda v, cb=cb, ob=ob: (v.tensor_sub(cb, SQT, ob), v.drain())[0],
               waits=wlo, inc=("sVE", 1))
            own_lo_ready[j] = P.c("sVE")
            sq_free_v = P.c("sVE")
            wtr = [("sSC", own_hi_ready[j]), ("sVE", pbt_free)]
            for hc in range(8):
                em(TE, lambda t, hc=hc, ob=ob: t.transpose(
                    PBT[:, hc * 128:(hc + 1) * 128], ob[:, hc * 128:(hc + 1) * 128],
                    IDB[:]), waits=(wtr if hc == 0 else []),
                   inc=(("sTE", 1) if hc == 7 else None))
            hi_mm[j] = P.c("sTE")
            em(V, lambda v, j=j: v.tensor_copy(
                XNFM[:, :].rearrange("p (hc t) -> p hc t", t=LC)[:, :, j * 128:(j + 1) * 128],
                PBT[:, 0:1024].rearrange("p (hc t) -> p hc t", t=128)),
               waits=[("sTE", hi_mm[j])], inc=("sVE", 1))
            hi_evac[j] = P.c("sVE")
            wtr2 = [("sVE", own_lo_ready[j]), ("sVE", hi_evac[j])]
            if j >= 1:
                wtr2.append(("sTE", score_ms[j - 1]))
            for hc in range(8):
                em(TE, lambda t, hc=hc, cb=cb: t.transpose(
                    PBT[:, hc * 128:(hc + 1) * 128], cb[:, hc * 128:(hc + 1) * 128],
                    IDB[:]), waits=(wtr2 if hc == 0 else []),
                   inc=(("sTE", 1) if hc == 7 else None))
            lo_mm[j] = P.c("sTE")
            em(V, lambda v: v.tensor_copy(XNLOF[:, :], PBT[:, 0:1024]),
               waits=[("sTE", lo_mm[j])], inc=("sVE", 1))
            pbt_free = P.c("sVE")
            wtr3 = [("sSC", ob2_ready), ("sVE", pbt_free)]
            for hc in range(8):
                em(TE, lambda t, hc=hc: t.transpose(
                    PBT[:, hc * 128:(hc + 1) * 128], OB2[:, hc * 128:(hc + 1) * 128],
                    IDB[:]), waits=(wtr3 if hc == 0 else []),
                   inc=(("sTE", 1) if hc == 7 else None))
            ob2_mm[j] = P.c("sTE")
            wxf = [("sTE", ob2_mm[j])]
            if j >= 1:
                wxf.append(("sTE", x2_mm[j - 1]))
            for hc in range(8):
                em(SC, lambda s, hc=hc: s.activation(
                    XF2[:, hc * 128:(hc + 1) * 128], PBT[:, hc * 128:(hc + 1) * 128],
                    AF.Identity, bias=BCOL[:, hc:hc + 1], scale=GCOL[:, hc:hc + 1]),
                   waits=(wxf if hc == 0 else []),
                   inc=(("sSC", 1) if hc == 7 else None))
            xf2_sc = P.c("sSC")
            wdx = [("sSC", xf2_sc)]
            if j >= 1:
                wdx.append(("sTE", dx_mm[j - 1]))
            def dxf(v):
                for hc in range(8):
                    v.tensor_scalar(DXF2[:, hc * 128:(hc + 1) * 128],
                                    XF2[:, hc * 128:(hc + 1) * 128],
                                    DROW[:, hc:hc + 1], None, OP.mult)
                return v.drain()
            em(V, dxf, waits=wdx, inc=("sVE", 1))
            dxf_ve = P.c("sVE")
            wx2t = [("sSC", xf2_sc), ("sVE", pbt_free)]
            for hc in range(8):
                em(TE, lambda t, hc=hc: t.transpose(
                    PBT[:, hc * 128:(hc + 1) * 128], XF2[:, hc * 128:(hc + 1) * 128],
                    IDB[:]), waits=(wx2t if hc == 0 else []),
                   inc=(("sTE", 1) if hc == 7 else None))
            x2_mm[j] = P.c("sTE")
            em(V, lambda v, j=j: v.tensor_copy(
                XN2TM[:, j * 1024:(j + 1) * 1024], PBT[:, 0:1024]),
               waits=[("sTE", x2_mm[j])], inc=("sVE", 1))
            xn2_ready[j] = P.c("sVE")
            wdxt = [("sVE", dxf_ve), ("sVE", xn2_ready[j])]
            for hc in range(8):
                em(TE, lambda t, hc=hc: t.transpose(
                    PBT[:, hc * 128:(hc + 1) * 128], DXF2[:, hc * 128:(hc + 1) * 128],
                    IDB[:]), waits=(wdxt if hc == 0 else []),
                   inc=(("sTE", 1) if hc == 7 else None))
            dx_mm[j] = P.c("sTE")
            em(V, lambda v, j=j, xs=xs: (v.tensor_add(
                ACC[:, j * 1024:(j + 1) * 1024], PBT[:, 0:1024], xs), v.drain())[0],
               waits=[("sTE", dx_mm[j])], inc=("sVE", 1))
            pbt_free = P.c("sVE")
            slot_v[slot] = P.c("sVE")
            slot_s[slot] = own_hi_ready[j]
            wsc2 = [("sVE", hi_evac[j]), ("sVE", P.c("sVE") if j == 0 else hi_evac[j])]
            wsc2 = [("sVE", pbt_free)]
            if j == 0 and sup_read is not None:
                wsc2.append(("sVE", sup_read))
            for ph in range(3):
                for hc in range(8):
                    if ph == 0:
                        lhsT = XNFM[:, hc * LC + j * 128: hc * LC + (j + 1) * 128]
                        rhs = GATEH[:, hc * E:(hc + 1) * E]
                    elif ph == 1:
                        lhsT = XNFM[:, hc * LC + j * 128: hc * LC + (j + 1) * 128]
                        rhs = GATEL[:, hc * E:(hc + 1) * E]
                    else:
                        lhsT = XNLOF[:, hc * 128:(hc + 1) * 128]
                        rhs = GATEH[:, hc * E:(hc + 1) * E]
                    first = (ph == 0 and hc == 0)
                    last = (ph == 2 and hc == 7)
                    em(TE, lambda t, lhsT=lhsT, rhs=rhs, j=j, first=first, last=last:
                       t.matmul(PS[2][:, 128 + j * 8:136 + j * 8], lhsT, rhs,
                                start=first, stop=last, skip_group_check=True),
                       waits=(wsc2 if first else []),
                       inc=(("sTE", 1) if last else None))
            score_ms[j] = P.c("sTE")

    if limit <= 1:
        return finish()
    em(V, lambda v: (v.tensor_copy(SCORES[:, :], PS[2][:, 128:256]), v.drain())[0],
       waits=[("sTE", score_ms[NOWN - 1]), ("sSC", P.c("sSC"))], inc=("sVE", 1))
    scores_ev = P.c("sVE")
    dma(SY, GBt, d_gamma.unsqueeze(0).partition_broadcast(128),
        waits=[("sVE", scores_ev), ("sSC", P.c("sSC"))])
    dma(SY, BBt, d_beta.unsqueeze(0).partition_broadcast(128))
    gb_ld = P.c("sX")

    def sin_fix(v):
        v.tensor_scalar(VTMP[0:64, :], BBt[0:64, :], SRS[0:64, 0:1], None, OP.mult)
        v.tensor_scalar(VTMP[64:128, :], BBt[64:128, :], SRS[64:128, 0:1], None, OP.mult)
        v.tensor_mul(SINR[:, :], SINR[:, :], GBt)
        v.drain()
        v.tensor_add(SINR[:, :], SINR[:, :], VTMP[:, :])
        return v.drain()
    em(V, sin_fix, waits=[("sVE", sup_read), ("sX", gb_ld)], inc=("sVE", 1))
    sin_done = P.c("sVE")

    def r3(ap):
        return ap.rearrange("p (g e) -> p g e", e=E)
    def routing1(v):
        s3 = r3(SCORES[:, :])
        m1v = RSM[:, 0:16].rearrange("p (g o) -> p g o", o=1)
        v.reduce_max(m1v, s3, axis=mybir.AxisListType.X)
        v.drain()
        m1b = m1v.broadcast_to([128, 16, E])
        v.tensor_scalar(RSC[:, 0:128], SCORES[:, :], -1.0, None, OP.mult)
        v.drain()
        v.tensor_max(RSC[:, 0:128], RSC[:, 0:128], SCORES[:, :])
        v.drain()
        v.tensor_tensor(r3(RSC[:, 128:256]), r3(RSC[:, 0:128]), m1b, OP.max)
        v.drain()
        v.tensor_scalar(RSC[:, 128:256], RSC[:, 128:256], -JIT2, None, OP.mult)
        v.drain()
        v.tensor_tensor(r3(RSC[:, 256:384]), s3, m1b, OP.subtract)
        v.drain()
        v.tensor_tensor(r3(RSC[:, 384:512]), r3(RSC[:, 256:384]), r3(RSC[:, 128:256]), OP.is_ge)
        v.drain()
        v.tensor_tensor(r3(RSC[:, 640:768]), s3, m1b, OP.is_equal)
        return v.drain()
    em(V, routing1, waits=[("sVE", scores_ev)], inc=("sVE", 1))
    em(SC, lambda s: s.activation(RSC[:, 512:640], RSC[:, 256:384], AF.Exp),
       waits=[("sVE", P.c("sVE"))], inc=("sSC", 1))
    m_e1 = P.c("sSC")
    def routing2(v):
        s3 = r3(SCORES[:, :])
        v.tensor_mul(RSC[:, 512:640], RSC[:, 512:640], RSC[:, 384:512])
        v.drain()
        z1v = RSM[:, 16:32].rearrange("p (g o) -> p g o", o=1)
        v.reduce_sum(z1v, r3(RSC[:, 512:640]), axis=mybir.AxisListType.X)
        v.drain()
        v.reciprocal(RSM[:, 32:48], RSM[:, 16:32])
        v.drain()
        v.scalar_tensor_tensor(RSC[:, 768:896], RSC[:, 640:768], -1.0e5,
                               SCORES[:, :], OP.mult, OP.add)
        v.drain()
        m2v = RSM[:, 48:64].rearrange("p (g o) -> p g o", o=1)
        v.reduce_max(m2v, r3(RSC[:, 768:896]), axis=mybir.AxisListType.X)
        v.drain()
        m2b = m2v.broadcast_to([128, 16, E])
        v.tensor_tensor(r3(RSC[:, 128:256]), r3(RSC[:, 0:128]), m2b, OP.max)
        v.drain()
        v.tensor_scalar(RSC[:, 128:256], RSC[:, 128:256], -JIT2, None, OP.mult)
        v.drain()
        v.tensor_tensor(r3(RSC[:, 256:384]), s3, m2b, OP.subtract)
        v.drain()
        v.tensor_tensor(r3(RSC[:, 384:512]), r3(RSC[:, 256:384]), r3(RSC[:, 128:256]), OP.is_ge)
        v.drain()
        v.tensor_scalar(RSC[:, 896:1024], RSC[:, 640:768], -1.0, 1.0, OP.mult, OP.add)
        v.drain()
        v.tensor_mul(RSC[:, 384:512], RSC[:, 384:512], RSC[:, 896:1024])
        v.drain()
        v.tensor_tensor(r3(RSC[:, 896:1024]), r3(RSC[:, 768:896]), m2b, OP.is_equal)
        return v.drain()
    em(V, routing2, waits=[("sSC", m_e1)], inc=("sVE", 1))
    em(SC, lambda s: s.activation(RSC[:, 512:640], RSC[:, 256:384], AF.Exp),
       waits=[("sVE", P.c("sVE"))], inc=("sSC", 1))
    m_e2 = P.c("sSC")
    def routing3(v):
        v.tensor_mul(RSC[:, 512:640], RSC[:, 512:640], RSC[:, 384:512])
        v.drain()
        z2v = RSM[:, 64:80].rearrange("p (g o) -> p g o", o=1)
        v.reduce_sum(z2v, r3(RSC[:, 512:640]), axis=mybir.AxisListType.X)
        v.drain()
        v.reciprocal(RSM[:, 80:96], RSM[:, 64:80])
        v.drain()
        m1b = RSM[:, 32:48].rearrange("p (g o) -> p g o", o=1).broadcast_to([128, 16, E])
        m2b = RSM[:, 80:96].rearrange("p (g o) -> p g o", o=1).broadcast_to([128, 16, E])
        v.tensor_tensor(r3(RSC[:, 0:128]), r3(RSC[:, 640:768]), m1b, OP.mult)
        v.drain()
        v.tensor_tensor(r3(RSC[:, 128:256]), r3(RSC[:, 896:1024]), m2b, OP.mult)
        v.drain()
        v.tensor_add(WTOKSB[:, :], RSC[:, 0:128], RSC[:, 128:256])
        return v.drain()
    em(V, routing3, waits=[("sSC", m_e2)], inc=("sVE", 1))
    wtok_done = P.c("sVE")
    em(V, lambda v: (v.tensor_copy(WTOKD, WTOKSB[:, :]), v.drain())[0],
       waits=[("sVE", wtok_done)], inc=("sVE", 1))
    wtok_stash = P.c("sVE")
    em(V, lambda v: (v.tensor_copy(SR_[:, :], SINR[0:64, :]),
                     v.tensor_copy(SI_[:, :], SINR[64:128, :]), v.drain())[-1],
       waits=[("sVE", wtok_stash)], inc=("sVE", 1))
    sin_done = P.c("sVE")
    if DEBUG_TAPS:
        em(SY, lambda s: s.dma_start(out=d_tap4, in_=WTOKSB[:, :]),
           waits=[("sVE", wtok_done)], inc=("sST", 16))
        em(SY, lambda s: s.dma_start(out=d_tap5, in_=SCORES[:, :]),
           waits=[("sVE", wtok_done)], inc=("sST", 16))

    x1_ms = {}
    cs_ms = {}
    last_rec = sin_done
    last_cs_consumed = score_ms[NOWN - 1]
    last_y_ev = ve_k32
    last_cx_consumed = None
    ydb_ms = {}
    ydb_consumed = {}
    pbt_c_used = pbt_free
    X2C = POOLB[:, 24496:26544]

    for jg in range(8):
        for rnd in range(2):
            wtr = [("sVE", pbt_c_used)]
            for q in range(4):
                hc = rnd * 4 + q
                for jj in range(2):
                    em(TE, lambda t, jg=jg, jj=jj, hc=hc, q=q: t.transpose(
                        PBT[:, q * 256 + jj * 128:q * 256 + (jj + 1) * 128],
                        XN2TM[:, (2 * jg + jj) * 1024 + hc * 128:
                              (2 * jg + jj) * 1024 + (hc + 1) * 128],
                        IDB[:]), waits=(wtr if (q == 0 and jj == 0) else []),
                       inc=(("sTE", 1) if (q == 3 and jj == 1) else None))
            x2t_mt = P.c("sTE")
            em(V, lambda v, rnd=rnd: (v.tensor_copy(
                X2C[:, rnd * 1024:(rnd + 1) * 1024], PBT[:, 0:1024]),
                v.drain())[0], waits=[("sTE", x2t_mt)], inc=("sVE", 1))
            pbt_c_used = P.c("sVE")
        for hc in range(8):
            wv = []
            if jg == 0 and hc == 0:
                wv.append(("sVE", sin_done))
            if jg > 0 and hc == 0:
                wv.append(("sTE", ydb_consumed[jg - 1]))
            def diag(v, hc=hc):
                src_ = X2C[:, hc * 256:(hc + 1) * 256]
                dst = YD32[:, hc * 256:(hc + 1) * 256]
                s4 = src_.rearrange("p (j i k) -> p j i k", j=2, k=Q2)
                d4 = dst.rearrange("p (j i k) -> p j i k", j=2, k=Q2)
                v.tensor_scalar(d4, s4, K32[:, hc * Q2:hc * Q2 + 1], None, OP.mult)
                v.drain()
                for d in range(1, Q2):
                    v.scalar_tensor_tensor(
                        d4[:, :, :, d:Q2], s4[:, :, :, 0:Q2 - d],
                        K32[:, hc * Q2 + d:hc * Q2 + d + 1],
                        d4[:, :, :, d:Q2], OP.mult, OP.add)
                    v.drain()
                return v.drain()
            em(V, diag, waits=wv, inc=("sVE", 1))
        ydb_ms[jg] = P.c("sVE")

        for jj in range(2):
            j = 2 * jg + jj
            par = j % 2
            psx1 = (PS[3], PS[4]) if par == 0 else (PS[5], PS[6])
            wx1 = [("sVE", xn2_ready[j])]
            if j >= 2:
                wx1.append(("sVE", rec_j_ms[j - 2]))
            em(TE, lambda t, j=j, psx1=psx1: t.matmul(
                psx1[0][:, :], A1PK[:, :], XN2TM[:, j * 1024:j * 1024 + 512],
                start=True, stop=True, skip_group_check=True), waits=wx1)
            em(TE, lambda t, j=j, psx1=psx1: t.matmul(
                psx1[1][:, :], A1PK[:, :], XN2TM[:, j * 1024 + 512:(j + 1) * 1024],
                start=True, stop=True, skip_group_check=True), inc=("sTE", 1))
            x1_ms[j] = P.c("sTE")

            wcs = [("sVE", last_rec), ("sTE", last_cs_consumed)]
            def cs_ops(v):
                v.tensor_mul(VTMP[0:NST, :], SR_[:, :], CNR[:, :])
                v.tensor_mul(SQT[0:NST, :], SI_[:, :], CNI[:, :])
                v.drain()
                v.tensor_sub(CSR[:, :], VTMP[0:NST, :], SQT[0:NST, :])
                v.drain()
                v.tensor_mul(VTMP[0:NST, :], SR_[:, :], CNI[:, :])
                v.tensor_mul(SQT[0:NST, :], SI_[:, :], CNR[:, :])
                v.drain()
                v.tensor_add(CSI[:, :], VTMP[0:NST, :], SQT[0:NST, :])
                return v.drain()
            em(V, cs_ops, waits=wcs, inc=("sVE", 1))
            cs_ms[j] = P.c("sVE")

            def rec_ops(v, psx1=psx1):
                v.tensor_scalar(VTMP[0:64, :], SR_[:, :], EQC[:, 0:1], None, OP.mult)
                v.tensor_scalar(SQT[0:64, :], SI_[:, :], EQC[:, 0:1], None, OP.mult)
                v.drain()
                v.scalar_tensor_tensor(VTMP[0:64, :], SI_[:, :], EQC[:, 2:3],
                                       VTMP[0:64, :], OP.mult, OP.add)
                v.scalar_tensor_tensor(SQT[0:64, :], SR_[:, :], EQC[:, 1:2],
                                       SQT[0:64, :], OP.mult, OP.add)
                v.drain()
                v.tensor_add(SR_[:, 0:512], VTMP[0:64, 0:512], psx1[0][0:64, :])
                v.tensor_add(SR_[:, 512:1024], VTMP[0:64, 512:1024], psx1[1][0:64, :])
                v.tensor_add(SI_[:, 0:512], SQT[0:64, 0:512], psx1[0][64:128, :])
                v.tensor_add(SI_[:, 512:1024], SQT[0:64, 512:1024], psx1[1][64:128, :])
                return v.drain()
            em(V, rec_ops, waits=[("sTE", x1_ms[j]), ("sVE", cs_ms[j])], inc=("sVE", 1))
            rec_j_ms[j] = P.c("sVE")
            last_rec = rec_j_ms[j]

            cx_ms = []
            for ip in range(3):
                wx2 = [("sVE", xn2_ready[j])]
                if ip == 0 and j == 0:
                    wx2.append(("sVE", sup_read))
                if ip > 0:
                    wx2.append(("sVE", cx_ms[ip - 1]))
                elif last_cx_consumed is not None:
                    wx2.append(("sTE", last_cx_consumed))
                em(TE, lambda t, j=j, ip=ip: t.matmul(
                    PS[0][:, :], A2L[:, ip * 128:(ip + 1) * 128],
                    XN2TM[:, j * 1024:j * 1024 + 512],
                    start=True, stop=True, skip_group_check=True), waits=wx2)
                em(TE, lambda t, j=j, ip=ip: t.matmul(
                    PS[1][:, :], A2L[:, ip * 128:(ip + 1) * 128],
                    XN2TM[:, j * 1024 + 512:(j + 1) * 1024],
                    start=True, stop=True, skip_group_check=True), inc=("sTE", 1))
                mt = P.c("sTE")
                def cx_ops(v, ip=ip):
                    for hh in range(2):
                        ph = PS[hh]
                        v.tensor_mul(VTMP[0:64, hh * 512:(hh + 1) * 512], ph[0:64, :],
                                     CNR[:, hh * 512:(hh + 1) * 512])
                        v.tensor_mul(SQT[0:64, hh * 512:(hh + 1) * 512], ph[64:128, :],
                                     CNI[:, hh * 512:(hh + 1) * 512])
                    v.drain()
                    v.tensor_sub(CX2R[:, ip * H:(ip + 1) * H], VTMP[0:64, :], SQT[0:64, :])
                    v.drain()
                    for hh in range(2):
                        ph = PS[hh]
                        v.tensor_mul(VTMP[0:64, hh * 512:(hh + 1) * 512], ph[0:64, :],
                                     CNI[:, hh * 512:(hh + 1) * 512])
                        v.tensor_mul(SQT[0:64, hh * 512:(hh + 1) * 512], ph[64:128, :],
                                     CNR[:, hh * 512:(hh + 1) * 512])
                    v.drain()
                    v.tensor_add(CX2I[:, ip * H:(ip + 1) * H],
                                 VTMP[0:64, :], SQT[0:64, :])
                    return v.drain()
                em(V, cx_ops, waits=[("sTE", mt)], inc=("sVE", 1))
                cx_ms.append(P.c("sVE"))

            for hh in range(2):
                wy = [("sVE", cs_ms[j]), ("sVE", cx_ms[2]), ("sVE", ydb_ms[jg]),
                      ("sVE", last_y_ev)]
                em(TE, lambda t, hh=hh: t.matmul(
                    PS[2][:, :], G1RB[:, :], CSR[:, hh * 512:(hh + 1) * 512],
                    start=True, stop=False, skip_group_check=True), waits=wy)
                em(TE, lambda t, hh=hh: t.matmul(
                    PS[2][:, :], G1INB[:, :], CSI[:, hh * 512:(hh + 1) * 512],
                    start=False, stop=False, skip_group_check=True))
                for m in (1, 2, 3):
                    em(TE, lambda t, hh=hh, m=m: t.matmul(
                        PS[2][:, :], G1S[:, (m - 1) * T:m * T],
                        CX2R[:, (m - 1) * H + hh * 512:(m - 1) * H + (hh + 1) * 512],
                        start=False, stop=False, skip_group_check=True))
                    em(TE, lambda t, hh=hh, m=m: t.matmul(
                        PS[2][:, :], G1S[:, (2 + m) * T:(3 + m) * T],
                        CX2I[:, (m - 1) * H + hh * 512:(m - 1) * H + (hh + 1) * 512],
                        start=False, stop=False, skip_group_check=True))
                for q in range(4):
                    hc = hh * 4 + q
                    em(TE, lambda t, jj=jj, hc=hc, q=q: t.matmul(
                        PS[2][:, q * 128:(q + 1) * 128],
                        YD32[:, hc * 256 + jj * 128:hc * 256 + (jj + 1) * 128],
                        IDF[:], start=False, stop=(q == 3), skip_group_check=True),
                       inc=(("sTE", 1) if q == 3 else None))
                mt = P.c("sTE")
                em(V, lambda v, j=j, hh=hh: (v.tensor_add(
                    ACC[:, j * H + hh * 512:j * H + (hh + 1) * 512],
                    ACC[:, j * H + hh * 512:j * H + (hh + 1) * 512], PS[2][:, :]),
                    v.drain())[0],
                   waits=[("sTE", mt)], inc=("sVE", 1))
                last_y_ev = P.c("sVE")
            last_cs_consumed = mt
            last_cx_consumed = mt
        ydb_consumed[jg] = P.c("sTE")

    if limit <= 3:
        return finish()
    fbseq = [(e, fb) for e in range(min(E, dlim)) for fb in range(8)]
    wm13 = {}
    wm2 = {}
    slot_last_mm = {}

    def emit_w13(idx):
        e, fb = fbseq[idx]
        slot = idx % 3
        if idx < 3:
            w = [("sTE", P.c("sTE")), ("sVE", P.c("sVE")), ("sSC", P.c("sSC"))]
        else:
            w = [("sTE", slot_last_mm[idx - 3])]
        wsem = f"sW{slot}"
        dma(GP, W1B[:, slot * 4096:(slot + 1) * 4096].rearrange(
            "p (hc f) -> p hc f", f=512),
            d_w1[e].rearrange("(hc p) (fb f) -> fb p hc f", p=128, f=512)[fb],
            waits=w, sem=wsem)
        dma(GP, W3B[:, slot * 4096:(slot + 1) * 4096].rearrange(
            "p (hc f) -> p hc f", f=512),
            d_w3[e].rearrange("(hc p) (fb f) -> fb p hc f", p=128, f=512)[fb],
            sem=wsem)
        wm13[idx] = (wsem, P.c(wsem))

    def emit_w2(idx):
        e, fb = fbseq[idx]
        slot = idx % 2
        if idx < 2:
            w = [("sTE", P.c("sTE")), ("sVE", P.c("sVE")), ("sSC", P.c("sSC"))]
        else:
            w = [("sTE", slot_last_mm[idx - 2])]
        wsem = "sV2a" if slot == 0 else "sV2b"
        dma(GP, W2B[:, slot * 4096:(slot + 1) * 4096].rearrange(
            "p (fc h) -> p fc h", h=1024),
            d_w2[e].rearrange("(fb fc p) h -> fb p fc h", p=128, fc=4)[fb],
            waits=w, sem=wsem)
        wm2[idx] = (wsem, P.c(wsem))

    par_free_s = [None, None]
    par_free_v = [None, None]
    gsb_free = [None, None]
    moe_start_ve = P.c("sVE")
    emit_w13(0)
    emit_w13(1)
    emit_w13(2)
    emit_w2(0)
    emit_w2(1)
    wrot_free = [moe_start_ve, moe_start_ve, moe_start_ve]
    wrot_ix = 0
    gv_ms = {}
    acc_final = None

    pairs = [(e, p, e * 8 + 2 * p, e * 8 + 2 * p + 1)
             for e in range(min(E, dlim)) for p in range(4)]

    def g_block(pi, tb):
        e, p, ifb0, ifb1 = pairs[pi]
        s0, s1 = ifb0 % 3, ifb1 % 3
        gbuf = GSB[:, (tb % 2) * 4096:(tb % 2) * 4096 + 4096]
        for half, (ifb, sl) in enumerate(((ifb0, s0), (ifb1, s1))):
            for fsub in range(4):
                par = fsub % 2
                ph1 = PS[0] if par == 0 else PS[2]
                ph3 = PS[1] if par == 0 else PS[3]
                wg = [wm13[ifb], ("sVE", moe_start_ve)]
                if par_free_s[par] is not None:
                    wg.append(("sSC", par_free_s[par]))
                if par_free_v[par] is not None:
                    wg.append(("sVE", par_free_v[par]))
                for hc in range(8):
                    em(TE, lambda t, sl=sl, hc=hc, fsub=fsub, tb=tb, ph1=ph1:
                       t.matmul(ph1[:, :],
                                W1B[:, sl * 4096 + hc * 512 + fsub * 128:
                                    sl * 4096 + hc * 512 + (fsub + 1) * 128],
                                XNFM[:, hc * LC + tb * 512:hc * LC + (tb + 1) * 512],
                                start=(hc == 0), stop=(hc == 7),
                                skip_group_check=True),
                       waits=(wg if hc == 0 else []))
                for hc in range(8):
                    em(TE, lambda t, sl=sl, hc=hc, fsub=fsub, tb=tb, ph3=ph3:
                       t.matmul(ph3[:, :],
                                W3B[:, sl * 4096 + hc * 512 + fsub * 128:
                                    sl * 4096 + hc * 512 + (fsub + 1) * 128],
                                XNFM[:, hc * LC + tb * 512:hc * LC + (tb + 1) * 512],
                                start=(hc == 0), stop=(hc == 7),
                                skip_group_check=True),
                       inc=(("sTE", 1) if hc == 7 else None))
                mt = P.c("sTE")
                em(SC, lambda s, par=par, ph1=ph1: (s.activation(
                    SILT[:, par * 512:(par + 1) * 512], ph1[:, :], AF.Silu),
                    s.drain())[0],
                   waits=[("sTE", mt)], inc=("sSC", 1))
                par_free_s[par] = P.c("sSC")
                wgv = [("sSC", par_free_s[par]), ("sVE", moe_start_ve)]
                if gsb_free[tb % 2] is not None:
                    wgv.append(("sTE", gsb_free[tb % 2]))
                gcol = (half * 4 + fsub) * 512
                def gmul(v, par=par, ph3=ph3, gbuf=gbuf, gcol=gcol):
                    v.tensor_mul(gbuf[:, gcol:gcol + 512],
                                 SILT[:, par * 512:(par + 1) * 512],
                                 ph3[:, :])
                    return v.drain()
                em(V, gmul, waits=wgv, inc=("sVE", 1))
                par_free_v[par] = P.c("sVE")
                gv_ms[(e, 2 * p + half, tb, fsub)] = P.c("sVE")

    def w_block(pi, tb):
        nonlocal wrot_ix, acc_final
        e, p, ifb0, ifb1 = pairs[pi]
        gbuf = GSB[:, (tb % 2) * 4096:(tb % 2) * 4096 + 4096]
        for tw in range(2):
            for hhh in range(2):
                for tt2 in range(2):
                    tt = tw * 2 + tt2
                    rot = 4 + (wrot_ix % 3)
                    wrot_ix += 1
                    ww = [("sVE", gv_ms[(e, 2 * p + 1, tb, 3)]),
                          ("sVE", wrot_free[rot - 4]),
                          wm2[ifb0], wm2[ifb1]]
                    for fc in range(8):
                        half, fsub = fc // 4, fc % 4
                        sl = half
                        em(TE, lambda t, sl=sl, fc=fc, half=half, fsub=fsub,
                           tt=tt, hhh=hhh, rot=rot, gbuf=gbuf:
                           t.matmul(PS[rot][:, :],
                                    gbuf[:, (half * 4 + fsub) * 512 + tt * 128:
                                         (half * 4 + fsub) * 512 + (tt + 1) * 128],
                                    W2B[:, sl * 4096 + fsub * 1024 + hhh * 512:
                                        sl * 4096 + fsub * 1024 + (hhh + 1) * 512],
                                    start=(fc == 0), stop=(fc == 7),
                                    skip_group_check=True),
                           waits=(ww if fc == 0 else []),
                           inc=(("sTE", 1) if fc == 7 else None))
                    mt = P.c("sTE")
                    slot_last_mm[ifb0] = mt
                    slot_last_mm[ifb1] = mt
                    col = (tb * 4 + tt) * H + hhh * 512
                    wcol = (tb * 4 + tt) * E + e
                    em(V, lambda v, col=col, rot=rot, wcol=wcol: (
                        v.scalar_tensor_tensor(
                            ACC[:, col:col + 512], PS[rot][:, :],
                            WTOKD[:, wcol:wcol + 1],
                            ACC[:, col:col + 512], OP.mult, OP.add),
                        v.drain())[0],
                       waits=[("sTE", mt)], inc=("sVE", 1))
                    wrot_free[rot - 4] = P.c("sVE")
                    acc_final = P.c("sVE")
        gsb_free[tb % 2] = P.c("sTE")

    NB = len(pairs) * 4
    g_block(0, 0)
    for k in range(NB):
        pi, tb = k // 4, k % 4
        if k + 1 < NB:
            npi, ntb = (k + 1) // 4, (k + 1) % 4
            if ntb == 0:
                if1b = pairs[pi][3]
                for nxt in (if1b + 2, if1b + 3):
                    if nxt < len(fbseq) and nxt not in wm13:
                        emit_w13(nxt)
            g_block(npi, ntb)
        w_block(pi, tb)
        if tb == 3:
            if1b = pairs[pi][3]
            for nxt in (if1b + 1, if1b + 2):
                if nxt < len(fbseq) and nxt not in wm2:
                    emit_w2(nxt)

    em(SY, lambda s: s.dma_start(out=d_out.rearrange("(j p) h -> p j h", p=128),
                                 in_=ACC[:, :].rearrange("p (j h) -> p j h", h=H)),
       waits=[("sVE", acc_final)], inc=("sST", 16))
    em(SY, lambda s: s.wait_ge(sems["sST"], P.c("sST")))

    with nc.Block() as block:
        def mk(engname):
            def run(eng):
                for w, fn, inc in P.q[engname]:
                    for s, v in w:
                        eng.wait_ge(sems[s], v)
                    ins = fn(eng)
                    if inc is not None:
                        assert ins is not None, f"{engname} op needs inst for inc"
                        ins.then_inc(sems[inc[0]], inc[1])
            return run
        block.sync(mk("sync"))
        block.gpsimd(mk("gpsimd"))
        block.scalar(mk("scalar"))
        block.vector(mk("vector"))
        block.tensor(mk("tensor"))

    return nc


_NC_CACHE = [None]


def kernel(L0_attn_x, L0_attn_gamma, L0_attn_beta, L0_attn_LR, L0_attn_LI,
           L0_attn_CR, L0_attn_CI, L0_attn_D, L0_ffn_gate_w, L0_ffn_w1,
           L0_ffn_w2, L0_ffn_w3):
    x = np.ascontiguousarray(np.asarray(L0_attn_x, dtype=np.float32)[0])
    gate_wT = np.ascontiguousarray(np.asarray(L0_ffn_gate_w, np.float32).T)
    w1T = np.ascontiguousarray(np.asarray(L0_ffn_w1, np.float32).transpose(0, 2, 1))
    w2T = np.ascontiguousarray(np.asarray(L0_ffn_w2, np.float32).transpose(0, 2, 1))
    w3T = np.ascontiguousarray(np.asarray(L0_ffn_w3, np.float32).transpose(0, 2, 1))
    com = {
        "gate_wT": gate_wT, "w1T": w1T, "w2T": w2T, "w3T": w3T,
        "CR": np.ascontiguousarray(np.asarray(L0_attn_CR, np.float32)),
        "CI": np.ascontiguousarray(np.asarray(L0_attn_CI, np.float32)),
        "gamma": np.asarray(L0_attn_gamma, np.float32),
        "beta": np.asarray(L0_attn_beta, np.float32),
        "Dvec": np.asarray(L0_attn_D, np.float32),
        "LR": np.asarray(L0_attn_LR, np.float32),
        "LI": np.asarray(L0_attn_LI, np.float32),
    }
    in_maps = []
    for k in range(NCORE):
        npad = L - LC - LC * k
        xs = np.concatenate([
            np.zeros((npad, H), np.float32),
            x[0:LC * k],
            x[LC * k:LC * (k + 1)],
        ], axis=0)
        mask = np.zeros((128, NPRE), np.float32)
        first_real = NPRE - (LC * k) // T
        mask[:, first_real:] = 1.0
        m = dict(com)
        m["x_shift"] = xs
        m["mask_pre"] = mask
        in_maps.append(m)

    if _NC_CACHE[0] is None:
        _NC_CACHE[0] = build_program()
    nc = _NC_CACHE[0]
    res = run_bass_kernel_spmd(nc, in_maps, core_ids=list(range(NCORE)))
    out = np.concatenate([np.asarray(res.results[k]["out"], np.float32)
                          for k in range(NCORE)], axis=0)
    return out.reshape(1, L, H)


if __name__ == "__main__":
    nc = build_program()
    print("build ok")
